# version 17
# baseline (speedup 1.0000x reference)
import numpy as np
import ml_dtypes
import concourse.bass as bass
import concourse.mybir as mybir
from concourse.bass_utils import run_bass_kernel_spmd

F32 = mybir.dt.float32
BF16 = mybir.dt.bfloat16
AX = mybir.AxisListType.X
ALU = mybir.AluOpType
AF = mybir.ActivationFunctionType
BFNP = ml_dtypes.bfloat16

N = 4096
NB = 32
NA = 128
RB = 256
NRB = 4
RPC = RB * NRB
NCORE = 8
FBLK = 512
FD = NB * RB
EPS = 1e-5

ENGINES = ("sync", "gpsimd", "tensor", "vector", "scalar")


def _build_tables():
    a = np.arange(NA)[:, None].astype(np.float64)
    c = np.arange(NA)[None, :].astype(np.float64)
    m1r = np.zeros((NB, NA, NA), np.float64)
    m1i = np.zeros((NB, NA, NA), np.float64)
    m2r = np.zeros((NB, NA, NA), np.float64)
    m2i_n = np.zeros((NB, NA, NA), np.float64)
    for b in range(NB):
        th = 2 * np.pi * (a * c / 128.0 + b * c / 4096.0)
        m1r[b] = np.cos(th)
        m1i[b] = -np.sin(th)
        m2r[b] = (np.cos(th) / N).T
        m2i_n[b] = (-np.sin(th) / N).T
    bb = np.arange(NB)[:, None].astype(np.float64)
    dd = np.arange(NB)[None, :].astype(np.float64)
    w = 2 * np.pi * bb * dd / 32.0

    def bd(m):
        out = np.zeros((128, 128))
        for g in range(4):
            out[32 * g:32 * g + 32, 32 * g:32 * g + 32] = m
        return out

    t = {
        "m1r": m1r, "m1i": m1i,
        "m2r": m2r, "m2i_n": m2i_n,
        "bdr": bd(np.cos(w)), "bdi": bd(-np.sin(w)), "bdi_n": bd(np.sin(w)),
        "bii": bd(np.sin(w)), "bii_n": bd(-np.sin(w)),
    }
    return {k: np.ascontiguousarray(v).astype(BFNP) for k, v in t.items()}


_TABLES = _build_tables()

COMPUTE_SEMS = {"tP", "tV", "tA", "tG", "tS"}


class Prog:

    def __init__(self, nc, sems):
        self.nc = nc
        self.sems = sems
        self.cnt = {k: 0 for k in sems}
        self.steps = {e: [] for e in ENGINES}
        self.bufs = {}
        self.clock = {e: {} for e in ENGINES}
        self.hist = {s: {} for s in sems}
        self.selfw = {e: 0 for e in ENGINES}

    def _dep(self, key):
        return self.bufs.setdefault(key, {"w": None, "r": []})

    def step(self, eng, fn, reads=(), writes=(), extra_wait=(), dma=False,
             dsem=None, risky=False, war_reads=(), note=""):
        if dma:
            assert dsem is not None, "DMA steps need a dedicated semaphore"
            sem = dsem
        elif eng == "gpsimd":
            sem = "tG"
        else:
            sem = {"tensor": "tP", "vector": "tV", "scalar": "tA",
                   "sync": "tS"}[eng]
        inc = 16 if dma else 1
        waits = {}
        myclock = self.clock[eng]

        def add_wait(s, v, force=False):
            if s not in COMPUTE_SEMS:
                v = self.cnt[s]
            if v <= 0:
                return
            if s == sem and s in COMPUTE_SEMS:
                if force and v > self.selfw[eng]:
                    waits[s] = max(waits.get(s, 0), v)
                    self.selfw[eng] = v
                return
            if myclock.get(s, 0) >= v:
                return
            waits[s] = max(waits.get(s, 0), v)
            snap = self.hist[s].get(v)
            if snap:
                for s2, v2 in snap.items():
                    if myclock.get(s2, 0) < v2:
                        myclock[s2] = v2
            myclock[s] = v

        for k in reads:
            d = self._dep(k)
            if d["w"] is not None:
                s, v = d["w"]
                add_wait(s, v, force=d.get("risky", False))
        for k in writes:
            d = self._dep(k)
            if d["w"] is not None:
                add_wait(*d["w"])
            for s, v in d["r"]:
                add_wait(s, v)
        for s, v in extra_wait:
            add_wait(s, v)

        self.cnt[sem] += inc
        mycount = self.cnt[sem]
        snap = dict(self.clock[eng])
        snap[sem] = mycount
        self.hist[sem][mycount] = snap
        self.steps[eng].append((dict(waits), fn, sem, inc, note))
        for k in writes:
            d = self._dep(k)
            d["w"] = (sem, mycount)
            d["risky"] = risky
            d["r"] = []
        for k in reads:
            self._dep(k)["r"].append((sem, mycount))
        for k in war_reads:
            self._dep(k)["r"].append((sem, mycount))
        return (sem, mycount)

    def replay(self, eng, e):
        for waits, fn, sem, inc, note in self.steps[eng]:
            for s, v in waits.items():
                e.wait_ge(self.sems[s], v)
            ins = fn(e)
            if ins is None:
                ins = e.nop()
            ins.then_inc(self.sems[sem], inc)


def build_program(dev_taps=False):
    nc = bass.Bass("TRN2", target_bir_lowering=False, debug=False)

    dr = {}
    dr["xs1"] = nc.dram_tensor("xs1", [NA, NB, RPC], F32, kind="ExternalInput").ap()
    dr["wth"] = nc.dram_tensor("wth", [32, 128, NB * 128], F32, kind="ExternalInput").ap()
    dr["gatep"] = nc.dram_tensor("gatep", [NB, 128, 8], F32, kind="ExternalInput").ap()
    dr["icsp"] = nc.dram_tensor("icsp", [NB, 8, 128], F32, kind="ExternalInput").ap()
    dr["ocsp"] = nc.dram_tensor("ocsp", [32, 8, 128], F32, kind="ExternalInput").ap()
    dr["biasc"] = nc.dram_tensor("biasc", [128, 32], F32, kind="ExternalInput").ap()
    dr["idf"] = nc.dram_tensor("idf", [128, 128], F32, kind="ExternalInput").ap()
    for k in ("m1r", "m1i", "m2r", "m2i_n"):
        dr[k] = nc.dram_tensor(k, [NB, 128, 128], BF16, kind="ExternalInput").ap()
    for k in ("bdr", "bdi", "bdi_n", "bii", "bii_n"):
        dr[k] = nc.dram_tensor(k, [128, 128], BF16, kind="ExternalInput").ap()
    dr["sdram"] = nc.dram_tensor("sdram", [NB, NA, RPC], F32, kind="Internal").ap()
    dr["outp"] = nc.dram_tensor("outp", [32, 128, RPC], F32, kind="ExternalOutput").ap()

    taps = {}

    def mk_tap(name, shape, dtype):
        taps[name] = nc.dram_tensor("tap_" + name, shape, dtype,
                                    kind="ExternalOutput").ap()
        return taps[name]

    from contextlib import ExitStack
    ctx = ExitStack()

    def sbt(name, shape, dtype):
        return ctx.enter_context(nc.sbuf_tensor(name, shape, dtype))

    def pst(name, shape, dtype=F32):
        return ctx.enter_context(nc.psum_tensor(name, shape, dtype))

    with ctx:
        xs1b = sbt("xs1b", [128, NB * RB], BF16)
        SL = [sbt(f"sl{i}", [128, FD], BF16) for i in range(8)]
        m1r_s = sbt("m1r_s", [128, NB * 128], BF16)
        m1i_s = sbt("m1i_s", [128, NB * 128], BF16)
        m2r_s = sbt("m2r_s", [128, NB * 128], BF16)
        m2i_ns = sbt("m2i_ns", [128, NB * 128], BF16)
        bd_s = {k: sbt(k + "_s", [128, 128], BF16)
                for k in ("bdr", "bdi", "bdi_n", "bii", "bii_n")}
        ones_col = sbt("ones_col", [128, 1], BF16)
        ones_row = sbt("ones_row", [1, 128], F32)
        idf = sbt("idf_s", [128, 128], F32)
        gate_s = sbt("gate_s", [128, NB * 8], F32)
        bias_s = sbt("bias_s", [128, 32], F32)
        m8 = sbt("m8", [128, 16], F32)
        mrow = sbt("mrow", [1, 256], F32)
        trow = sbt("trow", [1, 256], F32)
        m1b = sbt("m1b", [128, RB], BF16)
        ttb = sbt("ttb", [128, RB], BF16)
        rsb = sbt("rsb", [128, RB], F32)
        srow = sbt("srow", [1, 512], F32)
        srow2 = sbt("srow2", [1, 512], F32)
        srow3 = sbt("srow3", [1, 512], F32)
        mb1 = sbt("mb1", [128, RB], F32)
        rb1 = sbt("rb1", [128, RB], F32)
        sstage = [sbt(f"sst{i}", [128, RB], F32) for i in range(2)]
        ics_t = [sbt(f"ics{i}", [8, 128], F32) for i in range(2)]
        ocs_t = [sbt(f"ocs{i}", [8, 128], F32) for i in range(2)]
        schunk = [sbt(f"sch{i}", [128, 1024], F32) for i in range(2)]
        logT = sbt("logT", [128, 32], F32)
        negmax = sbt("negmax", [128, 1], F32)
        essum = sbt("essum", [128, 1], F32)
        routTs = [sbt("routT0", [8, 512], F32), sbt("routT1", [8, 512], F32)]
        routTM = sbt("routTM", [128, 8], F32)
        def sl7f32(k):
            return SL[7][:, k * 1024:(k + 1) * 1024].bitcast(F32)
        meanb = sl7f32(0)
        rstdb = sl7f32(1)
        os_sb = [[sl7f32(2), sl7f32(3)], [sl7f32(4), sl7f32(5)]]
        outst = [sl7f32(2), sl7f32(3)]
        sqtmp2 = [SL[7][:, 6 * 1024:6 * 1024 + 512],
                  SL[7][:, 6 * 1024 + 512:6 * 1024 + 1024]]
        logF = SL[7][:, 7 * 1024:7 * 1024 + 1024].bitcast(F32)[0:8, :]

        bank = [pst(f"pb{i}", [128, 512], F32) for i in range(8)]
        BK = [f"bank{i}" for i in range(8)]

        sem_names = ["tP", "tV", "tA", "tG", "tS", "dCN", "dX", "dST",
                     "dSC0", "dSC1", "dSC2", "dSC3", "dIC0", "dIC1", "dOC0", "dOC1",
                     "dWT0", "dWT1", "dOT", "dOT2", "dST2", "dTP"]
        sems = {s: ctx.enter_context(nc.semaphore(s)) for s in sem_names}
        block = ctx.enter_context(nc.Block())
        P = Prog(nc, sems)

        def ld(name, dst, src_ap):
            P.step("sync", lambda e: e.dma_start(out=dst, in_=src_ap),
                   writes=(name,), dma=True, dsem="dCN", note="ld_" + name)

        ld("m1r", m1r_s[:].rearrange("p (b c) -> p b c", b=NB),
           dr["m1r"].rearrange("b p c -> p b c"))
        ld("m1i", m1i_s[:].rearrange("p (b c) -> p b c", b=NB),
           dr["m1i"].rearrange("b p c -> p b c"))
        ld("m2r", m2r_s[:].rearrange("p (b c) -> p b c", b=NB),
           dr["m2r"].rearrange("b p c -> p b c"))
        ld("m2i_n", m2i_ns[:].rearrange("p (b c) -> p b c", b=NB),
           dr["m2i_n"].rearrange("b p c -> p b c"))
        for k in bd_s:
            ld(k, bd_s[k][:], dr[k][:])
        ld("gate", gate_s[:].rearrange("p (b e) -> p b e", b=NB),
           dr["gatep"].rearrange("b p e -> p b e"))
        ld("bias", bias_s[:], dr["biasc"][:])
        ld("idf", idf[:], dr["idf"][:])
        P.step("vector", lambda e: e.memset(ones_col[:], 1.0), writes=("ones_col",))
        P.step("vector", lambda e: e.memset(ones_row[:], 1.0), writes=("ones_row",))

        def m1r_b(b): return m1r_s[:, b * 128:(b + 1) * 128]
        def m1i_b(b): return m1i_s[:, b * 128:(b + 1) * 128]
        def m2r_b(b): return m2r_s[:, b * 128:(b + 1) * 128]
        def m2i_nb(b): return m2i_ns[:, b * 128:(b + 1) * 128]

        def evac(par, dst_ap, src_ap, reads, writes, note=""):
            if par % 2 == 0:
                P.step("vector", lambda e: e.tensor_copy(dst_ap, src_ap),
                       reads=reads, writes=writes, note=note)
            else:
                P.step("scalar", lambda e: e.copy(dst_ap, src_ap),
                       reads=reads, writes=writes, note=note)

        def fwd_gen(src_keys_fn, src_ap_fn, iA1, iA2, iSP, tag):
            A1r, A1i = SL[iA1[0]], SL[iA1[1]]
            A2r, A2i = SL[iA2[0]], SL[iA2[1]]
            Xr, Xi = SL[iSP[0]], SL[iSP[1]]
            k_a1r, k_a1i = f"sl{iA1[0]}", f"sl{iA1[1]}"
            k_a2r, k_a2i = f"sl{iA2[0]}", f"sl{iA2[1]}"
            k_xr, k_xi = f"sl{iSP[0]}", f"sl{iSP[1]}"
            for b in range(NB):
                h = (b % 2) * RB
                ib = (b // 2) % 3
                pr, kr = bank[ib][:, h:h + RB], BK[ib]
                pi, ki = bank[3 + ib][:, h:h + RB], BK[3 + ib]
                P.step("tensor", lambda e, b=b, pr=pr: e.matmul(
                    pr, m1r_b(b), src_ap_fn(b), start=True, stop=True),
                    reads=("m1r",) + src_keys_fn(b), writes=(kr,), note=tag + "s1r")
                P.step("tensor", lambda e, b=b, pi=pi: e.matmul(
                    pi, m1i_b(b), src_ap_fn(b), start=True, stop=True),
                    reads=("m1i",) + src_keys_fn(b), writes=(ki,), note=tag + "s1i")
                if b % 2 == 1:
                    prf = bank[ib][:, 0:2 * RB]
                    pif = bank[3 + ib][:, 0:2 * RB]
                    evac(0, A1r[:, (b - 1) * RB:(b + 1) * RB], prf, (kr,),
                         (k_a1r,), tag + "e1r")
                    evac(1, A1i[:, (b - 1) * RB:(b + 1) * RB], pif, (ki,),
                         (k_a1i,), tag + "e1i")
            yield
            RQ = RB // 4
            for p4 in range(4):
                rsl4 = slice(p4 * RQ, (p4 + 1) * RQ)
                osl4 = slice(p4 * RQ * NB, (p4 + 1) * RQ * NB)
                P.step("vector", lambda e, rsl4=rsl4, osl4=osl4: e.transpose(
                    A2r[:, osl4],
                    A1r[:].rearrange("p (b r) -> p r b", b=NB)[:, rsl4, :]),
                    reads=(k_a1r,), writes=(k_a2r, k_a2r + f"_p{p4}"),
                    risky=True, note=tag + "vt_r")
                P.step("vector", lambda e, rsl4=rsl4, osl4=osl4: e.transpose(
                    A2i[:, osl4],
                    A1i[:].rearrange("p (b r) -> p r b", b=NB)[:, rsl4, :]),
                    reads=(k_a1i,), writes=(k_a2i, k_a2i + f"_p{p4}"),
                    risky=True, note=tag + "vt_i")
            yield
            ROT_RE = (4, 5, 0, 1)
            ROT_IM = (6, 7, 2, 3)
            for ch in range(FD // 512):
                s = slice(ch * 512, (ch + 1) * 512)
                pc = ch // 4
                kar = k_a2r + f"_p{pc}"
                kai = k_a2i + f"_p{pc}"
                pr, kr = bank[ROT_RE[ch % 4]][:, 0:512], BK[ROT_RE[ch % 4]]
                pi, ki = bank[ROT_IM[ch % 4]][:, 0:512], BK[ROT_IM[ch % 4]]
                P.step("tensor", lambda e, s=s, pr=pr: e.matmul(
                    pr, bd_s["bdr"][:], A2r[:, s], start=True, stop=False),
                    reads=("bdr", kar), war_reads=(k_a2r,), writes=(kr,))
                P.step("tensor", lambda e, s=s, pr=pr: e.matmul(
                    pr, bd_s["bdi_n"][:], A2i[:, s], start=False, stop=True),
                    reads=("bdi_n", kai), war_reads=(k_a2i,), writes=(kr,))
                P.step("tensor", lambda e, s=s, pi=pi: e.matmul(
                    pi, bd_s["bdi"][:], A2r[:, s], start=True, stop=False),
                    reads=("bdi", kar), war_reads=(k_a2r,), writes=(ki,))
                P.step("tensor", lambda e, s=s, pi=pi: e.matmul(
                    pi, bd_s["bdr"][:], A2i[:, s], start=False, stop=True),
                    reads=("bdr", kai), war_reads=(k_a2i,), writes=(ki,))
                evac(0, Xr[:, s], pr, (kr,), (k_xr,), tag + "e2r")
                evac(1, Xi[:, s], pi, (ki,), (k_xi,), tag + "e2i")

        def fwd_transform(src_slot_keys, src_ap_fn, iA1, iA2, iSP, tag):
            for _ in fwd_gen(lambda b: src_slot_keys, src_ap_fn, iA1, iA2,
                             iSP, tag):
                pass

        def inv_s1_shuffle(Yr_ap, Yi_ap, yr_keys, yi_keys, iB, iB2, iB3,
                           real_in, tag):
            Br, Bi = SL[iB[0]], SL[iB[1]]
            B2r, B2i = SL[iB2[0]], SL[iB2[1]]
            B3r, B3i = SL[iB3[0]], SL[iB3[1]]
            k_br, k_bi = f"sl{iB[0]}", f"sl{iB[1]}"
            k_b2r, k_b2i = f"sl{iB2[0]}", f"sl{iB2[1]}"
            k_b3r, k_b3i = f"sl{iB3[0]}", f"sl{iB3[1]}"
            ROT_RE = (4, 5, 0, 1)
            ROT_IM = (6, 7, 2, 3)
            for ch in range(FD // 512):
                s = slice(ch * 512, (ch + 1) * 512)
                pr, kr = bank[ROT_RE[ch % 4]][:, 0:512], BK[ROT_RE[ch % 4]]
                pi, ki = bank[ROT_IM[ch % 4]][:, 0:512], BK[ROT_IM[ch % 4]]
                if real_in:
                    P.step("tensor", lambda e, s=s, pr=pr: e.matmul(
                        pr, bd_s["bdr"][:], Yr_ap[:, s], start=True, stop=True),
                        reads=("bdr",) + yr_keys, writes=(kr,))
                    P.step("tensor", lambda e, s=s, pi=pi: e.matmul(
                        pi, bd_s["bii"][:], Yr_ap[:, s], start=True, stop=True),
                        reads=("bii",) + yr_keys, writes=(ki,))
                else:
                    P.step("tensor", lambda e, s=s, pr=pr: e.matmul(
                        pr, bd_s["bdr"][:], Yr_ap[:, s], start=True, stop=False),
                        reads=("bdr",) + yr_keys, writes=(kr,))
                    P.step("tensor", lambda e, s=s, pr=pr: e.matmul(
                        pr, bd_s["bii_n"][:], Yi_ap[:, s], start=False, stop=True),
                        reads=("bii_n",) + yi_keys, writes=(kr,))
                    P.step("tensor", lambda e, s=s, pi=pi: e.matmul(
                        pi, bd_s["bii"][:], Yr_ap[:, s], start=True, stop=False),
                        reads=("bii",) + yr_keys, writes=(ki,))
                    P.step("tensor", lambda e, s=s, pi=pi: e.matmul(
                        pi, bd_s["bdr"][:], Yi_ap[:, s], start=False, stop=True),
                        reads=("bdr",) + yi_keys, writes=(ki,))
                evac(0, Br[:, s], pr, (kr,), (k_br, k_br + f"_c{ch}"), tag + "e1r")
                evac(1, Bi[:, s], pi, (ki,), (k_bi, k_bi + f"_c{ch}"), tag + "e1i")
            for p4 in range(4):
                osl4 = slice(p4 * 2048, (p4 + 1) * 2048)
                krd = tuple(k_br + f"_c{c}" for c in range(4 * p4, 4 * p4 + 4))
                kid = tuple(k_bi + f"_c{c}" for c in range(4 * p4, 4 * p4 + 4))
                P.step("vector", lambda e, osl4=osl4: e.transpose(
                    B2r[:, osl4], Br[:, osl4]),
                    reads=krd, war_reads=(k_br,),
                    writes=(k_b2r,), risky=True, note=tag + "vt_r")
                P.step("vector", lambda e, osl4=osl4: e.transpose(
                    B2i[:, osl4], Bi[:, osl4]),
                    reads=kid, war_reads=(k_bi,),
                    writes=(k_b2i,), risky=True, note=tag + "vt_i")
            srcr = B2r[:].rearrange("p (r b) -> p b r", b=NB)
            dstr = B3r[:].rearrange("p (b r) -> p b r", b=NB)
            srci = B2i[:].rearrange("p (r b) -> p b r", b=NB)
            dsti = B3i[:].rearrange("p (b r) -> p b r", b=NB)
            for q in range(4):
                bq = slice(q * 8, (q + 1) * 8)
                P.step("vector", lambda e, bq=bq, srcr=srcr, dstr=dstr: e.tensor_copy(
                    dstr[:, bq, :], srcr[:, bq, :]),
                    reads=(k_b2r,), writes=(k_b3r, k_b3r + f"_q{q}"),
                    note=tag + "cpv")
                P.step("gpsimd", lambda e, bq=bq, srci=srci, dsti=dsti: e.tensor_copy(
                    dsti[:, bq, :], srci[:, bq, :]),
                    reads=(k_b2i,), writes=(k_b3i, k_b3i + f"_q{q}"),
                    note=tag + "cpp")
            return (k_b3r, k_b3i)

        xv = xs1b[:].rearrange("p (b r) -> p b r", b=NB)

        def emit_x_load(rb, half):
            bsl = slice(half * 16, half * 16 + 16)
            P.step("gpsimd", lambda e, rb=rb, bsl=bsl: e.dma_start(
                out=xs1b[:].rearrange("p (b r) -> p b r", b=NB)[:, bsl, :],
                in_=dr["xs1"][:, bsl, rb * RB:(rb + 1) * RB]),
                writes=(f"xs1_h{half}",), dma=True, dsem="dX",
                note=f"rb{rb}_ld_x{half}")

        def xsrc(b):
            return xv[:, b, :]

        def xkeys(b):
            return (f"xs1_h{b // 16}",)

        gx = None
        for rb in range(NRB):
            tag = f"rb{rb}_"
            if gx is None:
                emit_x_load(rb, 0)
                emit_x_load(rb, 1)
                gx = fwd_gen(xkeys, xsrc, (0, 1), (2, 3), (4, 5), tag + "fx")
                next(gx)
            for _ in gx:
                pass
            gx = None
            if dev_taps and rb == 0:
                t = mk_tap("Xr", [128, FD], BF16)
                P.step("gpsimd", lambda e, t=t: e.dma_start(out=t[:], in_=SL[4][:]),
                       reads=("sl4",), dma=True, dsem="dTP")
                t = mk_tap("Xi", [128, FD], BF16)
                P.step("gpsimd", lambda e, t=t: e.dma_start(out=t[:], in_=SL[5][:]),
                       reads=("sl5",), dma=True, dsem="dTP")
            for ch in range(4):
                s = slice(ch * 2048, (ch + 1) * 2048)
                P.step("scalar", lambda e, s=s: e.activation(SL[0][:, s], SL[4][:, s], AF.Square),
                       reads=("sl4",), writes=("sl0",))
                P.step("scalar", lambda e, s=s: e.activation(SL[1][:, s], SL[5][:, s], AF.Square),
                       reads=("sl5",), writes=("sl1",))
                P.step("vector", lambda e, s=s: e.tensor_add(SL[6][:, s], SL[0][:, s], SL[1][:, s]),
                       reads=("sl0", "sl1"), writes=("sl6",))
            inv_s1_shuffle(SL[6][:], None, ("sl6",), (), (0, 1), (2, 3), (6, 7),
                           True, tag + "ci")
            b3rv = SL[6][:].rearrange("p (b r) -> p b r", b=NB)
            b3iv = SL[7][:].rearrange("p (b r) -> p b r", b=NB)
            for b in range(NB):
                h = (b % 2) * RB
                ib = (b // 2) % 3
                pr, kr = bank[ib][:, h:h + RB], BK[ib]
                P.step("tensor", lambda e, b=b, pr=pr: e.matmul(
                    pr, m2r_b(b), b3rv[:, b, :], start=True, stop=False),
                    reads=("m2r", f"sl6_q{b // 8}"), war_reads=("sl6",),
                    writes=(kr,))
                P.step("tensor", lambda e, b=b, pr=pr: e.matmul(
                    pr, m2i_nb(b), b3iv[:, b, :], start=False, stop=True),
                    reads=("m2i_n", f"sl7_q{b // 8}"), war_reads=("sl7",),
                    writes=(kr,))
                if b % 2 == 1:
                    prf = bank[ib][:, 0:2 * RB]
                    evac(0, SL[0][:, (b - 1) * RB:(b + 1) * RB], prf, (kr,),
                         ("sl0",), "cFM")
                jb = 3 + (b % 3)
                for rs in range(2):
                    pt, kt = bank[jb][:, rs * 128:(rs + 1) * 128], BK[jb]
                    P.step("tensor", lambda e, b=b, rs=rs, pt=pt: e.matmul(
                        pt, b3rv[:, b, rs * 128:(rs + 1) * 128], m2r_b(b),
                        start=True, stop=False),
                        reads=("m2r", f"sl6_q{b // 8}"), war_reads=("sl6",),
                        writes=(kt,))
                    P.step("tensor", lambda e, b=b, rs=rs, pt=pt: e.matmul(
                        pt, b3iv[:, b, rs * 128:(rs + 1) * 128], m2i_nb(b),
                        start=False, stop=True),
                        reads=("m2i_n", f"sl7_q{b // 8}"), war_reads=("sl7",),
                        writes=(kt,))
                ptf = bank[jb][:, 0:256]
                dstTM = SL[1][:].rearrange("p (rs q) -> p rs q", rs=2)[
                    :, :, b * 128:(b + 1) * 128]
                P.step("vector", lambda e, b=b, ptf=ptf, dstTM=dstTM: e.tensor_copy(
                    dstTM, ptf[:].rearrange("p (rs q) -> p rs q", rs=2)),
                    reads=(BK[jb],), writes=("sl1",), note="cTM")
            if dev_taps and rb == 0:
                t = mk_tap("corrFM", [128, FD], BF16)
                P.step("gpsimd", lambda e, t=t: e.dma_start(out=t[:], in_=SL[0][:]),
                       reads=("sl0",), dma=True, dsem="dTP")
                t = mk_tap("corrTM", [128, FD], BF16)
                P.step("gpsimd", lambda e, t=t: e.dma_start(out=t[:], in_=SL[1][:]),
                       reads=("sl1",), dma=True, dsem="dTP")
            for rs in range(2):
                P.step("vector", lambda e, rs=rs: e.max(
                    out=m8[:, rs * 8:rs * 8 + 8], in_=SL[1][:, rs * 4096:(rs + 1) * 4096]),
                    reads=("sl1",), writes=("m8",), risky=True)
            for rs in range(2):
                pt, kt = bank[6][0:1, 0:128], BK[6]
                P.step("tensor", lambda e, rs=rs, pt=pt: e.transpose(
                    pt, m8[:, rs * 8:rs * 8 + 1], idf[:]),
                    reads=("m8", "idf"), writes=(kt,), note="tr_m1")
                P.step("vector", lambda e, rs=rs, pt=pt: e.tensor_copy(
                    mrow[0:1, rs * 128:(rs + 1) * 128], pt),
                    reads=(kt,), writes=("mrow",))
                pt2, kt2 = bank[7][0:1, 0:128], BK[7]
                P.step("tensor", lambda e, rs=rs, pt2=pt2: e.transpose(
                    pt2, m8[:, rs * 8 + 7:rs * 8 + 8], idf[:]),
                    reads=("m8", "idf"), writes=(kt2,), note="tr_t8")
                P.step("vector", lambda e, rs=rs, pt2=pt2: e.tensor_copy(
                    trow[0:1, rs * 128:(rs + 1) * 128], pt2),
                    reads=(kt2,), writes=("trow",))
            pm, km = bank[6][:, 0:RB], BK[6]
            P.step("tensor", lambda e, pm=pm: e.matmul(
                pm, ones_row[:], mrow[0:1, 0:RB], start=True, stop=True),
                reads=("mrow", "ones_row"), writes=(km,))
            P.step("vector", lambda e, pm=pm: e.tensor_copy(m1b[:], pm),
                   reads=(km,), writes=("m1b",))
            pm2, km2 = bank[7][:, 0:RB], BK[7]
            P.step("tensor", lambda e, pm2=pm2: e.matmul(
                pm2, ones_row[:], trow[0:1, 0:RB], start=True, stop=True),
                reads=("trow", "ones_row"), writes=(km2,))
            P.step("vector", lambda e, pm2=pm2: e.tensor_sub(ttb[:], pm2, m1b[:]),
                   reads=(km2, "m1b"), writes=("ttb",))
            m1bb = m1b[:].unsqueeze(1).to_broadcast([128, NB, RB])
            ttbb = ttb[:].unsqueeze(1).to_broadcast([128, NB, RB])
            cfmv = SL[0][:].rearrange("p (b r) -> p b r", b=NB)
            s2v = SL[2][:].rearrange("p (b r) -> p b r", b=NB)
            s3v = SL[3][:].rearrange("p (b r) -> p b r", b=NB)
            P.step("vector", lambda e: e.tensor_tensor(s3v, cfmv, m1bb, ALU.subtract),
                   reads=("sl0", "m1b"), writes=("sl3",), note="g_d")
            P.step("vector", lambda e: e.tensor_tensor(s2v, s3v, ttbb, ALU.is_ge),
                   reads=("sl3", "ttb"), writes=("sl2",), note="g_mask")
            P.step("scalar", lambda e: e.activation(SL[6][:], SL[3][:], AF.Exp),
                   reads=("sl3",), writes=("sl6",), note="g_exp")
            P.step("vector", lambda e: e.tensor_mul(SL[2][:], SL[2][:], SL[6][:]),
                   reads=("sl2", "sl6"), writes=("sl2",), note="g_mul")
            if dev_taps and rb == 0:
                t = mk_tap("g", [128, FD], BF16)
                P.step("gpsimd", lambda e, t=t: e.dma_start(out=t[:], in_=SL[2][:]),
                       reads=("sl2",), dma=True, dsem="dTP")
            psm, ksm = bank[7][0:1, 0:RB], BK[7]
            g2v = SL[2][:].rearrange("p (b r) -> p b r", b=NB)
            for b in range(NB):
                P.step("tensor", lambda e, b=b, psm=psm: e.matmul(
                    psm, ones_col[:], g2v[:, b, :], start=(b == 0), stop=(b == NB - 1)),
                    reads=("sl2", "ones_col"), writes=(ksm,))
            P.step("vector", lambda e, psm=psm: e.reciprocal(srow[0:1, 0:RB], psm),
                   reads=(ksm,), writes=("srow",), risky=True)
            pm3, km3 = bank[6][:, 0:RB], BK[6]
            P.step("tensor", lambda e, pm3=pm3: e.matmul(
                pm3, ones_row[:], srow[0:1, 0:RB], start=True, stop=True),
                reads=("srow", "ones_row"), writes=(km3,))
            P.step("vector", lambda e, pm3=pm3: e.tensor_copy(rsb[:], pm3),
                   reads=(km3,), writes=("rsb",))
            def gsrc(b):
                return g2v[:, b, :]

            fwd_transform(("sl2",), gsrc, (1, 3), (0, 6), (7, 2), tag + "fg")
            for ch in range(4):
                s = slice(ch * 2048, (ch + 1) * 2048)
                P.step("vector", lambda e, s=s: e.tensor_mul(SL[0][:, s], SL[4][:, s], SL[7][:, s]),
                       reads=("sl4", "sl7"), writes=("sl0",), note="y_t1")
                P.step("gpsimd", lambda e, s=s: e.tensor_mul(SL[1][:, s], SL[5][:, s], SL[2][:, s]),
                       reads=("sl5", "sl2"), writes=("sl1",), note="y_t2")
                P.step("vector", lambda e, s=s: e.tensor_mul(SL[3][:, s], SL[5][:, s], SL[7][:, s]),
                       reads=("sl5", "sl7"), writes=("sl3",), note="y_t3")
                P.step("gpsimd", lambda e, s=s: e.tensor_mul(SL[6][:, s], SL[4][:, s], SL[2][:, s]),
                       reads=("sl4", "sl2"), writes=("sl6",), note="y_t4")
                P.step("vector", lambda e, s=s: e.tensor_add(SL[0][:, s], SL[0][:, s], SL[1][:, s]),
                       reads=("sl0", "sl1"), writes=("sl0",), note="y_Yr")
                P.step("vector", lambda e, s=s: e.tensor_sub(SL[1][:, s], SL[3][:, s], SL[6][:, s]),
                       reads=("sl3", "sl6"), writes=("sl1",), note="y_Yi")
            inv_s1_shuffle(SL[0][:], SL[1][:], ("sl0",), ("sl1",), (2, 7), (3, 6),
                           (0, 1), False, tag + "yi")
            yb3rv = SL[0][:].rearrange("p (b r) -> p b r", b=NB)
            yb3iv = SL[1][:].rearrange("p (b r) -> p b r", b=NB)
            residv = SL[4][:].rearrange("p (b r) -> p b r", b=NB)
            for b in range(NB):
                pr, kr = bank[b % 3][:, 0:RB], BK[b % 3]
                P.step("tensor", lambda e, b=b, pr=pr: e.matmul(
                    pr, m2r_b(b), yb3rv[:, b, :], start=True, stop=False),
                    reads=("m2r", f"sl0_q{b // 8}"), war_reads=("sl0",),
                    writes=(kr,))
                P.step("tensor", lambda e, b=b, pr=pr: e.matmul(
                    pr, m2i_nb(b), yb3iv[:, b, :], start=False, stop=True),
                    reads=("m2i_n", f"sl1_q{b // 8}"), war_reads=("sl1",),
                    writes=(kr,))
                P.step("vector", lambda e, b=b, pr=pr: e.tensor_mul(
                    SL[3][:, b * RB:(b + 1) * RB], pr, rsb[:]),
                    reads=(kr, "rsb"), writes=("sl3",), note="res_y1")
                P.step("vector", lambda e, b=b: e.tensor_add(
                    residv[:, b, :], SL[3][:, b * RB:(b + 1) * RB], xv[:, b, :]),
                    reads=("sl3", f"xs1_h{b // 16}"), writes=(f"sl4b{b}",),
                    note="res_add")
                if rb + 1 < NRB and b in (15, NB - 1):
                    emit_x_load(rb + 1, b // 16)
            if dev_taps and rb == 0:
                t = mk_tap("resid", [128, FD], BF16)
                P.step("gpsimd", lambda e, t=t: e.dma_start(out=t[:], in_=SL[4][:]),
                       reads=tuple(f"sl4b{b}" for b in range(NB)),
                       dma=True, dsem="dTP")
            if rb + 1 < NRB:
                gx = fwd_gen(xkeys, xsrc, (0, 1), (2, 3), (4, 5),
                             f"rb{rb + 1}_fx")
                next(gx)
            psm1, ksm1 = bank[6][0:1, 0:RB], BK[6]
            psq1, ksq1 = bank[7][0:1, 0:RB], BK[7]
            for b in range(NB):
                P.step("tensor", lambda e, b=b, psm1=psm1: e.matmul(
                    psm1, ones_col[:], residv[:, b, :], start=(b == 0),
                    stop=(b == NB - 1)),
                    reads=(f"sl4b{b}", "ones_col"), war_reads=("sl4",),
                    writes=(ksm1,))
                P.step("scalar", lambda e, b=b: e.activation(
                    SL[6][:, b * RB:(b + 1) * RB], residv[:, b, :], AF.Square),
                    reads=(f"sl4b{b}",), war_reads=("sl4",),
                    writes=(f"sl6b{b}",) + (("sl6",) if b == 0 else ()),
                    note="res_sq")
                P.step("tensor", lambda e, b=b, psq1=psq1: e.matmul(
                    psq1, ones_col[:], SL[6][:, b * RB:(b + 1) * RB],
                    start=(b == 0), stop=(b == NB - 1)),
                    reads=(f"sl6b{b}", "ones_col"), war_reads=("sl6",),
                    writes=(ksq1,))
            P.step("vector", lambda e, psm1=psm1: e.tensor_scalar_mul(
                srow[0:1, 0:RB], psm1, 1.0 / N),
                reads=(ksm1,), writes=("srow",))
            P.step("vector", lambda e, psq1=psq1: e.tensor_scalar_mul(
                srow2[0:1, 0:RB], psq1, 1.0 / N),
                reads=(ksq1,), writes=("srow2",))
            P.step("vector", lambda e: e.tensor_mul(srow3[0:1, 0:RB], srow[0:1, 0:RB],
                                                    srow[0:1, 0:RB]),
                   reads=("srow",), writes=("srow3",))
            P.step("vector", lambda e: e.tensor_sub(srow2[0:1, 0:RB], srow2[0:1, 0:RB],
                                                    srow3[0:1, 0:RB]),
                   reads=("srow2", "srow3"), writes=("srow2",))
            P.step("vector", lambda e: e.tensor_scalar_add(
                srow2[0:1, 0:RB], srow2[0:1, 0:RB], EPS),
                reads=("srow2",), writes=("srow2",))
            P.step("scalar", lambda e: e.activation(srow2[0:1, 0:RB], srow2[0:1, 0:RB],
                                                    AF.Sqrt),
                   reads=("srow2",), writes=("srow2",))
            P.step("vector", lambda e: e.reciprocal(srow2[0:1, 0:RB], srow2[0:1, 0:RB]),
                   reads=("srow2",), writes=("srow2",), risky=True)
            if gx is not None:
                next(gx)
            pmB, kmB = bank[6][:, 0:RB], BK[6]
            P.step("tensor", lambda e, pmB=pmB: e.matmul(
                pmB, ones_row[:], srow[0:1, 0:RB], start=True, stop=True),
                reads=("srow", "ones_row"), writes=(kmB,))
            P.step("vector", lambda e, pmB=pmB: e.tensor_copy(mb1[:], pmB),
                   reads=(kmB,), writes=("mb1",))
            pmC, kmC = bank[7][:, 0:RB], BK[7]
            P.step("tensor", lambda e, pmC=pmC: e.matmul(
                pmC, ones_row[:], srow2[0:1, 0:RB], start=True, stop=True),
                reads=("srow2", "ones_row"), writes=(kmC,))
            P.step("vector", lambda e, pmC=pmC: e.tensor_copy(rb1[:], pmC),
                   reads=(kmC,), writes=("rb1",))
            sst4 = sstage + [SL[5][:, 0:512].bitcast(F32),
                             SL[5][:, 512:1024].bitcast(F32)]
            for b in range(NB):
                st = sst4[b % 4]
                kst = f"sst{b % 4}"
                wst = (kst, "sl5") if b in (2, 3) else (kst,)
                wrn = ("sl4",) if b % 4 < 2 else ("sl4", "sl5")
                P.step("vector", lambda e, b=b, st=st: e.tensor_sub(
                    st[:], residv[:, b, :], mb1[:]),
                    reads=(f"sl4b{b}", "mb1"), war_reads=("sl4",), writes=wst)
                P.step("vector", lambda e, st=st: e.tensor_mul(
                    st[:], st[:], rb1[:]),
                    reads=("rb1",), writes=(kst,))
                P.step("sync" if b % 2 == 0 else "gpsimd",
                       lambda e, b=b, st=st, rb=rb: e.dma_start(
                           out=dr["sdram"][b, :, rb * RB:(rb + 1) * RB], in_=st[:]),
                       reads=(kst,), war_reads=wrn[1:],
                       writes=(f"sd_{b}_{rb}",), dma=True,
                       dsem="dST" if b % 2 == 0 else "dST2")

        P.step("vector", lambda e: e.memset(SL[7][:, 0:2], 0.0),
               writes=("sl7", "sl7_mb", "sl7_rb", "sl7_os00", "sl7_os01",
                       "sl7_os10", "sl7_os11", "sl7_sq0", "sl7_sq1", "sl7_lf"),
               note="sl7_bridge")
        schunk4 = [schunk[0], schunk[1],
                   SL[4][:, 0:2048].bitcast(F32),
                   SL[4][:, 2048:4096].bitcast(F32)]
        for blk in range(2):
            fsl = slice(blk * FBLK, (blk + 1) * FBLK)
            routT, kroutT = routTs[blk], f"routT{blk}"
            psL, kL = bank[5][0:8, 0:512], BK[5]
            for b in range(NB):
                pair = b // 2
                sc = schunk4[pair % 4]
                ksc = f"sch{pair % 4}"
                wsc = (ksc, "sl4") if (blk == 0 and pair in (2, 3)) else (ksc,)
                if b % 2 == 0:
                    P.step("sync" if pair % 2 == 0 else "gpsimd",
                           lambda e, b=b, sc=sc, fsl=fsl: e.dma_start(
                        out=sc[:].rearrange("p (b r) -> p b r", b=2),
                        in_=dr["sdram"][b:b + 2, :, fsl].rearrange("b p r -> p b r")),
                        reads=(f"sd_{b}_{2 * blk}", f"sd_{b}_{2 * blk + 1}",
                               f"sd_{b + 1}_{2 * blk}", f"sd_{b + 1}_{2 * blk + 1}"),
                        writes=wsc, dma=True, dsem=f"dSC{pair % 4}")
                P.step("tensor", lambda e, b=b, sc=sc, psL=psL: e.matmul(
                    psL, gate_s[:, b * 8:(b + 1) * 8],
                    sc[:, (b % 2) * 512:(b % 2 + 1) * 512],
                    start=(b == 0), stop=(b == NB - 1)),
                    reads=(ksc, "gate"), writes=(kL,))
            P.step("vector", lambda e, psL=psL: e.tensor_copy(logF[:], psL),
                   reads=(kL,), writes=("sl7_lf",))
            for sub in range(4):
                pt, kt = bank[6][:, 0:8], BK[6]
                P.step("tensor", lambda e, sub=sub, pt=pt: e.transpose(
                    pt, logF[0:8, sub * 128:(sub + 1) * 128], idf[0:8, 0:8]),
                    reads=("sl7_lf", "idf"), writes=(kt,))
                P.step("vector", lambda e, sub=sub, pt=pt: e.tensor_copy(
                    logT[:, sub * 8:(sub + 1) * 8], pt),
                    reads=(kt,), writes=("logT",))
            for sub in range(4):
                lsl = slice(sub * 8, (sub + 1) * 8)
                P.step("vector", lambda e, lsl=lsl: e.reduce_max(
                    negmax[:], logT[:, lsl], axis=AX, negate=True),
                    reads=("logT",), writes=("negmax",), risky=True)
                P.step("scalar", lambda e, lsl=lsl: e.activation(
                    routTM[:, 0:8], logT[:, lsl], AF.Exp, bias=negmax[:, 0:1]),
                    reads=("logT", "negmax"), writes=("routTM",))
                P.step("vector", lambda e: e.reduce_sum(essum[:], routTM[:, 0:8], axis=AX),
                       reads=("routTM",), writes=("essum",), risky=True)
                P.step("vector", lambda e: e.reciprocal(essum[:], essum[:]),
                       reads=("essum",), writes=("essum",), risky=True)
                P.step("vector", lambda e: e.tensor_scalar_mul(
                    routTM[:, 0:8], routTM[:, 0:8], essum[:, 0:1]),
                    reads=("essum",), writes=("routTM",))
                pt2, kt2 = bank[6][0:8, 8:136], BK[6]
                P.step("tensor", lambda e, pt2=pt2: e.transpose(
                    pt2, routTM[:, 0:8], idf[:]),
                    reads=("routTM", "idf"), writes=(kt2,))
                P.step("vector", lambda e, sub=sub, pt2=pt2, routT=routT,
                       kroutT=kroutT: e.tensor_copy(
                    routT[:, sub * 128:(sub + 1) * 128], pt2),
                    reads=(kt2,), writes=(kroutT,))
            for b in range(NB):
                ic, kic = ics_t[b % 2], f"ics{b % 2}"
                P.step("sync", lambda e, b=b, ic=ic: e.dma_start(
                    out=ic[:], in_=dr["icsp"][b]),
                    writes=(kic,), dma=True, dsem=f"dIC{b % 2}")
                pair = b // 2
                sc = schunk4[pair % 4]
                ksc = f"sch{pair % 4}"
                if b % 2 == 0:
                    P.step("sync" if pair % 2 == 0 else "gpsimd",
                           lambda e, b=b, sc=sc, fsl=fsl: e.dma_start(
                        out=sc[:].rearrange("p (b r) -> p b r", b=2),
                        in_=dr["sdram"][b:b + 2, :, fsl].rearrange("b p r -> p b r")),
                        reads=(f"sd_{b}_{2 * blk}", f"sd_{b}_{2 * blk + 1}",
                               f"sd_{b + 1}_{2 * blk}", f"sd_{b + 1}_{2 * blk + 1}"),
                        writes=(ksc,), dma=True, dsem=f"dSC{pair % 4}")
                psI, kI = bank[b % 4][:, 0:512], BK[b % 4]
                P.step("tensor", lambda e, ic=ic, psI=psI, routT=routT: e.matmul(
                    psI, ic[:], routT[:], start=True, stop=True),
                    reads=(kic, kroutT), writes=(kI,))
                us = 2 * blk + b // 16
                ub = SL[us][:].rearrange("p (q r) -> p q r", q=16)[:, b % 16, :]
                P.step("vector", lambda e, b=b, ub=ub, sc=sc, psI=psI: e.tensor_mul(
                    ub, sc[:, (b % 2) * 512:(b % 2 + 1) * 512], psI),
                    reads=(ksc, kI), writes=(f"sl{us}",), note="u")

        def r2view(blk, oc):
            if blk == 0:
                sl, key = SL[5 + oc // 16], f"sl{5 + oc // 16}"
            elif oc < 16:
                sl, key = SL[4], "sl4"
            else:
                sl, key = xs1b, "xs1"
            return sl[:].rearrange("p (q r) -> p q r", q=16)[:, oc % 16, :], key

        def emit_wt_load(oc):
            if oc >= 32:
                return
            wt = m2r_s if oc % 2 == 0 else m2i_ns
            kwt = "m2r" if oc % 2 == 0 else "m2i_n"
            P.step("gpsimd", lambda e, oc=oc, wt=wt: e.dma_start(
                out=wt[:], in_=dr["wth"][oc]),
                writes=(kwt,), dma=True, dsem=f"dWT{oc % 2}", note="wt_ld")

        emit_wt_load(0)
        emit_wt_load(1)
        FROT = ((4, 5, 6), (2, 3, 7))
        for oc in range(32):
            wt = m2r_s if oc % 2 == 0 else m2i_ns
            kwt = "m2r" if oc % 2 == 0 else "m2i_n"
            wbv = m1r_s if oc % 2 == 0 else m1i_s
            kwb = "m1r" if oc % 2 == 0 else "m1i"
            P.step("scalar", lambda e, wt=wt, wbv=wbv: e.sign(wbv[:], wt[:]),
                   reads=(kwt,), writes=(kwb,), note="sign")
            emit_wt_load(oc + 2)
            ocb, koc = ocs_t[oc % 2], f"ocs{oc % 2}"
            P.step("sync", lambda e, oc=oc, ocb=ocb: e.dma_start(
                out=ocb[:], in_=dr["ocsp"][oc]),
                writes=(koc,), dma=True, dsem=f"dOC{oc % 2}")
            for blk in range(2):
                fsl = slice(blk * FBLK, (blk + 1) * FBLK)
                psO, kO = bank[blk][:, 0:512], BK[blk]
                P.step("tensor", lambda e, ocb=ocb, psO=psO, blk=blk: e.matmul(
                    psO, ocb[:], routTs[blk][:], start=True, stop=True),
                    reads=(koc, f"routT{blk}"), writes=(kO,))
                osb = os_sb[blk][oc % 2]
                kos = f"sl7_os{blk}{oc % 2}"
                P.step("vector", lambda e, psO=psO, osb=osb: e.tensor_copy(osb[:], psO),
                       reads=(kO,), writes=(kos,))
                snb = schunk[oc % 2][:, blk * 512:(blk + 1) * 512]
                ksn = f"sn{oc % 2}{blk}"
                wsn = (ksn, f"sch{oc % 2}") if oc < 2 else (ksn,)
                P.step("sync", lambda e, oc=oc, snb=snb, fsl=fsl: e.dma_start(
                    out=snb,
                    in_=dr["sdram"][:, oc * 4:(oc + 1) * 4, fsl].rearrange(
                        "b a r -> a b r")),
                    reads=tuple(f"sd_{b}_{r}" for b in range(NB)
                                for r in (2 * blk, 2 * blk + 1)),
                    writes=wsn, dma=True, dsem=f"dSC{oc % 2}", note="s_nat")
            for blk in range(2):
                psF, kF = bank[FROT[blk][oc % 3]][:, 0:512], BK[FROT[blk][oc % 3]]
                for b in range(NB):
                    us = 2 * blk + b // 16
                    ub = SL[us][:].rearrange("p (q r) -> p q r", q=16)[:, b % 16, :]
                    P.step("tensor", lambda e, b=b, wbv=wbv, ub=ub, psF=psF: e.matmul(
                        psF, wbv[:, b * 128:(b + 1) * 128], ub,
                        start=(b == 0), stop=(b == NB - 1)),
                        reads=(kwb, f"sl{us}"), writes=(kF,), note="ffn_mm")
                osb = os_sb[blk][oc % 2]
                kos = f"sl7_os{blk}{oc % 2}"
                r2v, kr2 = r2view(blk, oc)
                P.step("vector", lambda e, psF=psF, osb=osb, r2v=r2v: e.tensor_mul(
                    r2v, psF, osb[:]),
                    reads=(kF, kos), writes=(kr2,), note="f_os")
                snb = schunk[oc % 2][:, blk * 512:(blk + 1) * 512]
                ksn = f"sn{oc % 2}{blk}"
                P.step("vector", lambda e, oc=oc, snb=snb, r2v=r2v: e.scalar_tensor_tensor(
                    r2v, r2v, bias_s[:, oc:oc + 1], snb,
                    op0=ALU.add, op1=ALU.add),
                    reads=(ksn, "bias", kr2), writes=(kr2,), note="f_bias_res")

        psm2, ksm2 = bank[7][0:1, 0:512], BK[7]
        psq2, kq2 = bank[2][0:1, 0:512], BK[2]

        def ln2_stats(blk):
            for oc in range(32):
                r2v, kr2 = r2view(blk, oc)
                P.step("tensor", lambda e, oc=oc, r2v=r2v, psm2=psm2: e.matmul(
                    psm2, ones_col[:], r2v, start=(oc == 0), stop=(oc == 31)),
                    reads=(kr2, "ones_col"), writes=(ksm2,))
                sq = sqtmp2[oc % 2]
                ksq = f"sl7_sq{oc % 2}"
                P.step("scalar", lambda e, sq=sq, r2v=r2v: e.activation(sq, r2v, AF.Square),
                       reads=(kr2,), writes=(ksq,))
                P.step("tensor", lambda e, oc=oc, sq=sq, psq2=psq2: e.matmul(
                    psq2, ones_col[:], sq, start=(oc == 0), stop=(oc == 31)),
                    reads=(ksq, "ones_col"), writes=(kq2,))

        def ln2_scalars():
            P.step("vector", lambda e, psm2=psm2: e.tensor_scalar_mul(
                srow[0:1, :], psm2, 1.0 / N), reads=(ksm2,), writes=("srow",))
            P.step("vector", lambda e, psq2=psq2: e.tensor_scalar_mul(
                srow2[0:1, :], psq2, 1.0 / N), reads=(kq2,), writes=("srow2",))
            P.step("vector", lambda e: e.tensor_mul(srow3[0:1, :], srow[0:1, :],
                                                    srow[0:1, :]),
                   reads=("srow",), writes=("srow3",))
            P.step("vector", lambda e: e.tensor_sub(srow2[0:1, :], srow2[0:1, :],
                                                    srow3[0:1, :]),
                   reads=("srow2", "srow3"), writes=("srow2",))
            P.step("vector", lambda e: e.tensor_scalar_add(
                srow2[0:1, :], srow2[0:1, :], EPS),
                reads=("srow2",), writes=("srow2",))
            P.step("scalar", lambda e: e.activation(srow2[0:1, :], srow2[0:1, :],
                                                    AF.Sqrt),
                   reads=("srow2",), writes=("srow2",))
            P.step("vector", lambda e: e.reciprocal(srow2[0:1, :], srow2[0:1, :]),
                   reads=("srow2",), writes=("srow2",), risky=True)
            pmB, kmB = bank[3][:, 0:512], BK[3]
            P.step("tensor", lambda e, pmB=pmB: e.matmul(
                pmB, ones_row[:], srow[0:1, :], start=True, stop=True),
                reads=("srow", "ones_row"), writes=(kmB,))
            P.step("vector", lambda e, pmB=pmB: e.tensor_copy(meanb[:], pmB),
                   reads=(kmB,), writes=("sl7_mb",))
            pmC, kmC = bank[4][:, 0:512], BK[4]
            P.step("tensor", lambda e, pmC=pmC: e.matmul(
                pmC, ones_row[:], srow2[0:1, :], start=True, stop=True),
                reads=("srow2", "ones_row"), writes=(kmC,))
            P.step("vector", lambda e, pmC=pmC: e.tensor_copy(rstdb[:], pmC),
                   reads=(kmC,), writes=("sl7_rb",))

        def ln2_normalize(blk):
            fsl = slice(blk * FBLK, (blk + 1) * FBLK)
            for oc in range(32):
                r2v, kr2 = r2view(blk, oc)
                ot = os_sb[(oc % 4) // 2][oc % 2]
                kot = f"sl7_os{(oc % 4) // 2}{oc % 2}"
                P.step("vector", lambda e, r2v=r2v, ot=ot: e.tensor_sub(
                    ot[:], r2v, meanb[:]),
                    reads=(kr2, "sl7_mb"), writes=(kot,))
                P.step("vector", lambda e, ot=ot: e.tensor_mul(ot[:], ot[:], rstdb[:]),
                       reads=("sl7_rb",), writes=(kot,))
                P.step("sync" if oc % 2 == 0 else "gpsimd",
                       lambda e, oc=oc, ot=ot, fsl=fsl: e.dma_start(
                           out=dr["outp"][oc, :, fsl], in_=ot[:]),
                       reads=(kot,), writes=(f"out_{oc}_{blk}",), dma=True,
                       dsem="dOT" if oc % 2 == 0 else "dOT2")

        ln2_stats(0)
        ln2_scalars()
        ln2_stats(1)
        ln2_normalize(0)
        ln2_scalars()
        ln2_normalize(1)

        mx = {}
        for oc in range(32):
            for blk in range(2):
                s, v = P.bufs[f"out_{oc}_{blk}"]["w"]
                mx[s] = max(mx.get(s, 0), v)
        P.step("gpsimd", lambda e: e.nop(), extra_wait=tuple(mx.items()),
               note="final_wait")

        @block.sync
        def _(e):
            P.replay("sync", e)

        @block.gpsimd
        def _(e):
            P.replay("gpsimd", e)

        @block.tensor
        def _(e):
            P.replay("tensor", e)

        @block.vector
        def _(e):
            P.replay("vector", e)

        @block.scalar
        def _(e):
            P.replay("scalar", e)

    return nc, taps


def _host_prep(L0_attn_x, L0_ffn_weight, L0_ffn_bias, L0_ffn_gate_w,
               L0_ffn_in_channel_scale, L0_ffn_out_channel_scale):
    x = np.ascontiguousarray(np.asarray(L0_attn_x, np.float32).reshape(8192, N))
    w = np.asarray(L0_ffn_weight, np.float32)
    bias = np.asarray(L0_ffn_bias, np.float32)
    gate = np.asarray(L0_ffn_gate_w, np.float32)
    ics = np.asarray(L0_ffn_in_channel_scale, np.float32)
    ocs = np.asarray(L0_ffn_out_channel_scale, np.float32)

    wth = np.ascontiguousarray(
        w.reshape(32, 128, 128, 32).transpose(0, 2, 3, 1)).reshape(32, 128, NB * 128)
    gatep = np.ascontiguousarray(gate.T.reshape(128, 32, 8).transpose(1, 0, 2))
    icsp = np.ascontiguousarray(ics.reshape(8, 128, 32).transpose(2, 0, 1))
    ocsp = np.ascontiguousarray(ocs.reshape(8, 32, 128).transpose(1, 0, 2))
    biasc = np.ascontiguousarray(bias.reshape(32, 128).T)
    idf = np.eye(128, dtype=np.float32)
    shared = {"wth": wth, "gatep": gatep, "icsp": icsp, "ocsp": ocsp,
              "biasc": biasc, "idf": idf}
    shared.update(_TABLES)

    in_maps = []
    for ci in range(NCORE):
        xs = x[ci * RPC:(ci + 1) * RPC]
        xs1 = np.ascontiguousarray(xs.reshape(RPC, NA, NB).transpose(1, 2, 0))
        m = dict(shared)
        m["xs1"] = xs1
        in_maps.append(m)
    return in_maps


_PROG_CACHE = {}


def kernel(**inputs):
    if "prog" not in _PROG_CACHE:
        _PROG_CACHE["prog"] = build_program(dev_taps=False)
    nc, _ = _PROG_CACHE["prog"]
    in_maps = _host_prep(**inputs)
    res = run_bass_kernel_spmd(nc, in_maps, core_ids=list(range(NCORE)))
    outs = []
    for ci in range(NCORE):
        o = np.asarray(res.results[ci]["outp"])
        outs.append(o.reshape(N, RPC).T)
    return np.ascontiguousarray(np.concatenate(outs, axis=0).reshape(4, 2048, N))


# version 18 (final)
# speedup vs baseline: 1.0179x; 1.0179x over previous
import numpy as np
import ml_dtypes
import concourse.bass as bass
import concourse.mybir as mybir
from concourse.bass_utils import run_bass_kernel_spmd

F32 = mybir.dt.float32
BF16 = mybir.dt.bfloat16
AX = mybir.AxisListType.X
ALU = mybir.AluOpType
AF = mybir.ActivationFunctionType
BFNP = ml_dtypes.bfloat16

N = 4096
NB = 32
NA = 128
RB = 256
NRB = 4
RPC = RB * NRB
NCORE = 8
FBLK = 512
FD = NB * RB
EPS = 1e-5

ENGINES = ("sync", "gpsimd", "tensor", "vector", "scalar")


def _build_tables():
    a = np.arange(NA)[:, None].astype(np.float64)
    c = np.arange(NA)[None, :].astype(np.float64)
    m1r = np.zeros((NB, NA, NA), np.float64)
    m1i = np.zeros((NB, NA, NA), np.float64)
    m2r = np.zeros((NB, NA, NA), np.float64)
    m2i_n = np.zeros((NB, NA, NA), np.float64)
    for b in range(NB):
        th = 2 * np.pi * (a * c / 128.0 + b * c / 4096.0)
        m1r[b] = np.cos(th)
        m1i[b] = -np.sin(th)
        m2r[b] = (np.cos(th) / N).T
        m2i_n[b] = (-np.sin(th) / N).T
    bb = np.arange(NB)[:, None].astype(np.float64)
    dd = np.arange(NB)[None, :].astype(np.float64)
    w = 2 * np.pi * bb * dd / 32.0

    def bd(m):
        out = np.zeros((128, 128))
        for g in range(4):
            out[32 * g:32 * g + 32, 32 * g:32 * g + 32] = m
        return out

    t = {
        "m1r": m1r, "m1i": m1i,
        "m2r": m2r, "m2i_n": m2i_n,
        "bdr": bd(np.cos(w)), "bdi": bd(-np.sin(w)), "bdi_n": bd(np.sin(w)),
        "bii": bd(np.sin(w)), "bii_n": bd(-np.sin(w)),
    }
    return {k: np.ascontiguousarray(v).astype(BFNP) for k, v in t.items()}


_TABLES = _build_tables()

COMPUTE_SEMS = {"tP", "tV", "tA", "tG", "tS"}


class Prog:

    def __init__(self, nc, sems):
        self.nc = nc
        self.sems = sems
        self.cnt = {k: 0 for k in sems}
        self.steps = {e: [] for e in ENGINES}
        self.bufs = {}
        self.clock = {e: {} for e in ENGINES}
        self.hist = {s: {} for s in sems}
        self.selfw = {e: 0 for e in ENGINES}

    def _dep(self, key):
        return self.bufs.setdefault(key, {"w": None, "r": []})

    def step(self, eng, fn, reads=(), writes=(), extra_wait=(), dma=False,
             dsem=None, risky=False, war_reads=(), note=""):
        if dma:
            assert dsem is not None, "DMA steps need a dedicated semaphore"
            sem = dsem
        elif eng == "gpsimd":
            sem = "tG"
        else:
            sem = {"tensor": "tP", "vector": "tV", "scalar": "tA",
                   "sync": "tS"}[eng]
        inc = 16 if dma else 1
        waits = {}
        myclock = self.clock[eng]

        def add_wait(s, v, force=False):
            if s not in COMPUTE_SEMS:
                v = self.cnt[s]
            if v <= 0:
                return
            if s == sem and s in COMPUTE_SEMS:
                if force and v > self.selfw[eng]:
                    waits[s] = max(waits.get(s, 0), v)
                    self.selfw[eng] = v
                return
            if myclock.get(s, 0) >= v:
                return
            waits[s] = max(waits.get(s, 0), v)
            snap = self.hist[s].get(v)
            if snap:
                for s2, v2 in snap.items():
                    if myclock.get(s2, 0) < v2:
                        myclock[s2] = v2
            myclock[s] = v

        for k in reads:
            d = self._dep(k)
            if d["w"] is not None:
                s, v = d["w"]
                add_wait(s, v, force=d.get("risky", False))
        for k in writes:
            d = self._dep(k)
            if d["w"] is not None:
                add_wait(*d["w"])
            for s, v in d["r"]:
                add_wait(s, v)
        for s, v in extra_wait:
            add_wait(s, v)

        self.cnt[sem] += inc
        mycount = self.cnt[sem]
        snap = dict(self.clock[eng])
        snap[sem] = mycount
        self.hist[sem][mycount] = snap
        self.steps[eng].append((dict(waits), fn, sem, inc, note))
        for k in writes:
            d = self._dep(k)
            d["w"] = (sem, mycount)
            d["risky"] = risky
            d["r"] = []
        for k in reads:
            self._dep(k)["r"].append((sem, mycount))
        for k in war_reads:
            self._dep(k)["r"].append((sem, mycount))
        return (sem, mycount)

    def replay(self, eng, e):
        for waits, fn, sem, inc, note in self.steps[eng]:
            for s, v in waits.items():
                e.wait_ge(self.sems[s], v)
            ins = fn(e)
            if ins is None:
                ins = e.nop()
            ins.then_inc(self.sems[sem], inc)


def build_program(dev_taps=False):
    nc = bass.Bass("TRN2", target_bir_lowering=False, debug=False)

    dr = {}
    dr["xs1"] = nc.dram_tensor("xs1", [NA, NB, RPC], F32, kind="ExternalInput").ap()
    dr["wth"] = nc.dram_tensor("wth", [32, 128, NB * 128], F32, kind="ExternalInput").ap()
    dr["gatep"] = nc.dram_tensor("gatep", [NB, 128, 8], F32, kind="ExternalInput").ap()
    dr["icsp"] = nc.dram_tensor("icsp", [NB, 8, 128], F32, kind="ExternalInput").ap()
    dr["ocsp"] = nc.dram_tensor("ocsp", [32, 8, 128], F32, kind="ExternalInput").ap()
    dr["biasc"] = nc.dram_tensor("biasc", [128, 32], F32, kind="ExternalInput").ap()
    dr["idf"] = nc.dram_tensor("idf", [128, 128], F32, kind="ExternalInput").ap()
    for k in ("m1r", "m1i", "m2r", "m2i_n"):
        dr[k] = nc.dram_tensor(k, [NB, 128, 128], BF16, kind="ExternalInput").ap()
    for k in ("bdr", "bdi", "bdi_n", "bii", "bii_n"):
        dr[k] = nc.dram_tensor(k, [128, 128], BF16, kind="ExternalInput").ap()
    dr["sdram"] = nc.dram_tensor("sdram", [NB, NA, RPC], F32, kind="Internal").ap()
    dr["outp"] = nc.dram_tensor("outp", [32, 128, RPC], F32, kind="ExternalOutput").ap()

    taps = {}

    def mk_tap(name, shape, dtype):
        taps[name] = nc.dram_tensor("tap_" + name, shape, dtype,
                                    kind="ExternalOutput").ap()
        return taps[name]

    from contextlib import ExitStack
    ctx = ExitStack()

    def sbt(name, shape, dtype):
        return ctx.enter_context(nc.sbuf_tensor(name, shape, dtype))

    def pst(name, shape, dtype=F32):
        return ctx.enter_context(nc.psum_tensor(name, shape, dtype))

    with ctx:
        xs1b = sbt("xs1b", [128, NB * RB], BF16)
        SL = [sbt(f"sl{i}", [128, FD], BF16) for i in range(8)]
        m1r_s = sbt("m1r_s", [128, NB * 128], BF16)
        m1i_s = sbt("m1i_s", [128, NB * 128], BF16)
        m2r_s = sbt("m2r_s", [128, NB * 128], BF16)
        m2i_ns = sbt("m2i_ns", [128, NB * 128], BF16)
        bd_s = {k: sbt(k + "_s", [128, 128], BF16)
                for k in ("bdr", "bdi", "bdi_n", "bii", "bii_n")}
        ones_col = sbt("ones_col", [128, 1], BF16)
        ones_row = sbt("ones_row", [1, 128], F32)
        idf = sbt("idf_s", [128, 128], F32)
        gate_s = sbt("gate_s", [128, NB * 8], F32)
        bias_s = sbt("bias_s", [128, 32], F32)
        m8 = sbt("m8", [128, 16], F32)
        mrow = sbt("mrow", [1, 256], F32)
        trow = sbt("trow", [1, 256], F32)
        m1b = sbt("m1b", [128, RB], BF16)
        ttb = sbt("ttb", [128, RB], BF16)
        rsb = sbt("rsb", [128, RB], F32)
        srow = sbt("srow", [1, 512], F32)
        srow2 = sbt("srow2", [1, 512], F32)
        srow3 = sbt("srow3", [1, 512], F32)
        mb1 = sbt("mb1", [128, RB], F32)
        rb1 = sbt("rb1", [128, RB], F32)
        sstage = [sbt(f"sst{i}", [128, RB], F32) for i in range(2)]
        ics_t = [sbt(f"ics{i}", [8, 128], F32) for i in range(2)]
        ocs_t = [sbt(f"ocs{i}", [8, 128], F32) for i in range(2)]
        schunk = [sbt(f"sch{i}", [128, 1024], F32) for i in range(2)]
        logT = sbt("logT", [128, 32], F32)
        negmax = sbt("negmax", [128, 1], F32)
        essum = sbt("essum", [128, 1], F32)
        routTs = [sbt("routT0", [8, 512], F32), sbt("routT1", [8, 512], F32)]
        routTM = sbt("routTM", [128, 8], F32)
        def sl7f32(k):
            return SL[7][:, k * 1024:(k + 1) * 1024].bitcast(F32)
        meanb = sl7f32(0)
        rstdb = sl7f32(1)
        os_sb = [[sl7f32(2), sl7f32(3)], [sl7f32(4), sl7f32(5)]]
        outst = [sl7f32(2), sl7f32(3)]
        sqtmp2 = [SL[7][:, 6 * 1024:6 * 1024 + 512],
                  SL[7][:, 6 * 1024 + 512:6 * 1024 + 1024]]
        logF = SL[7][:, 7 * 1024:7 * 1024 + 1024].bitcast(F32)[0:8, :]

        bank = [pst(f"pb{i}", [128, 512], F32) for i in range(8)]
        BK = [f"bank{i}" for i in range(8)]

        sem_names = ["tP", "tV", "tA", "tG", "tS", "dCN", "dX0", "dX1", "dST",
                     "dSC0", "dSC1", "dSC2", "dSC3", "dIC0", "dIC1", "dOC0", "dOC1",
                     "dWT0", "dWT1", "dOT", "dOT2", "dST2", "dTP"]
        sems = {s: ctx.enter_context(nc.semaphore(s)) for s in sem_names}
        block = ctx.enter_context(nc.Block())
        P = Prog(nc, sems)

        def ld(name, dst, src_ap):
            P.step("sync", lambda e: e.dma_start(out=dst, in_=src_ap),
                   writes=(name,), dma=True, dsem="dCN", note="ld_" + name)

        ld("m1r", m1r_s[:].rearrange("p (b c) -> p b c", b=NB),
           dr["m1r"].rearrange("b p c -> p b c"))
        ld("m1i", m1i_s[:].rearrange("p (b c) -> p b c", b=NB),
           dr["m1i"].rearrange("b p c -> p b c"))
        ld("m2r", m2r_s[:].rearrange("p (b c) -> p b c", b=NB),
           dr["m2r"].rearrange("b p c -> p b c"))
        ld("m2i_n", m2i_ns[:].rearrange("p (b c) -> p b c", b=NB),
           dr["m2i_n"].rearrange("b p c -> p b c"))
        for k in bd_s:
            ld(k, bd_s[k][:], dr[k][:])
        ld("gate", gate_s[:].rearrange("p (b e) -> p b e", b=NB),
           dr["gatep"].rearrange("b p e -> p b e"))
        ld("bias", bias_s[:], dr["biasc"][:])
        ld("idf", idf[:], dr["idf"][:])
        P.step("vector", lambda e: e.memset(ones_col[:], 1.0), writes=("ones_col",))
        P.step("vector", lambda e: e.memset(ones_row[:], 1.0), writes=("ones_row",))

        def m1r_b(b): return m1r_s[:, b * 128:(b + 1) * 128]
        def m1i_b(b): return m1i_s[:, b * 128:(b + 1) * 128]
        def m2r_b(b): return m2r_s[:, b * 128:(b + 1) * 128]
        def m2i_nb(b): return m2i_ns[:, b * 128:(b + 1) * 128]

        def evac(par, dst_ap, src_ap, reads, writes, note=""):
            if par % 2 == 0:
                P.step("vector", lambda e: e.tensor_copy(dst_ap, src_ap),
                       reads=reads, writes=writes, note=note)
            else:
                P.step("scalar", lambda e: e.copy(dst_ap, src_ap),
                       reads=reads, writes=writes, note=note)

        def fwd_gen(src_keys_fn, src_ap_fn, iA1, iA2, iSP, tag):
            A1r, A1i = SL[iA1[0]], SL[iA1[1]]
            A2r, A2i = SL[iA2[0]], SL[iA2[1]]
            Xr, Xi = SL[iSP[0]], SL[iSP[1]]
            k_a1r, k_a1i = f"sl{iA1[0]}", f"sl{iA1[1]}"
            k_a2r, k_a2i = f"sl{iA2[0]}", f"sl{iA2[1]}"
            k_xr, k_xi = f"sl{iSP[0]}", f"sl{iSP[1]}"
            for b in range(NB):
                h = (b % 2) * RB
                ib = (b // 2) % 3
                pr, kr = bank[ib][:, h:h + RB], BK[ib]
                pi, ki = bank[3 + ib][:, h:h + RB], BK[3 + ib]
                P.step("tensor", lambda e, b=b, pr=pr: e.matmul(
                    pr, m1r_b(b), src_ap_fn(b), start=True, stop=True),
                    reads=("m1r",) + src_keys_fn(b), writes=(kr,), note=tag + "s1r")
                P.step("tensor", lambda e, b=b, pi=pi: e.matmul(
                    pi, m1i_b(b), src_ap_fn(b), start=True, stop=True),
                    reads=("m1i",) + src_keys_fn(b), writes=(ki,), note=tag + "s1i")
                if b % 2 == 1:
                    prf = bank[ib][:, 0:2 * RB]
                    pif = bank[3 + ib][:, 0:2 * RB]
                    evac(0, A1r[:, (b - 1) * RB:(b + 1) * RB], prf, (kr,),
                         (k_a1r,), tag + "e1r")
                    evac(1, A1i[:, (b - 1) * RB:(b + 1) * RB], pif, (ki,),
                         (k_a1i,), tag + "e1i")
            yield
            RQ = RB // 4
            for p4 in range(4):
                rsl4 = slice(p4 * RQ, (p4 + 1) * RQ)
                osl4 = slice(p4 * RQ * NB, (p4 + 1) * RQ * NB)
                P.step("vector", lambda e, rsl4=rsl4, osl4=osl4: e.transpose(
                    A2r[:, osl4],
                    A1r[:].rearrange("p (b r) -> p r b", b=NB)[:, rsl4, :]),
                    reads=(k_a1r,), writes=(k_a2r, k_a2r + f"_p{p4}"),
                    risky=True, note=tag + "vt_r")
                P.step("vector", lambda e, rsl4=rsl4, osl4=osl4: e.transpose(
                    A2i[:, osl4],
                    A1i[:].rearrange("p (b r) -> p r b", b=NB)[:, rsl4, :]),
                    reads=(k_a1i,), writes=(k_a2i, k_a2i + f"_p{p4}"),
                    risky=True, note=tag + "vt_i")
            yield
            ROT_RE = (4, 5, 0, 1)
            ROT_IM = (6, 7, 2, 3)
            for ch in range(FD // 512):
                s = slice(ch * 512, (ch + 1) * 512)
                pc = ch // 4
                kar = k_a2r + f"_p{pc}"
                kai = k_a2i + f"_p{pc}"
                pr, kr = bank[ROT_RE[ch % 4]][:, 0:512], BK[ROT_RE[ch % 4]]
                pi, ki = bank[ROT_IM[ch % 4]][:, 0:512], BK[ROT_IM[ch % 4]]
                P.step("tensor", lambda e, s=s, pr=pr: e.matmul(
                    pr, bd_s["bdr"][:], A2r[:, s], start=True, stop=False),
                    reads=("bdr", kar), war_reads=(k_a2r,), writes=(kr,))
                P.step("tensor", lambda e, s=s, pr=pr: e.matmul(
                    pr, bd_s["bdi_n"][:], A2i[:, s], start=False, stop=True),
                    reads=("bdi_n", kai), war_reads=(k_a2i,), writes=(kr,))
                P.step("tensor", lambda e, s=s, pi=pi: e.matmul(
                    pi, bd_s["bdi"][:], A2r[:, s], start=True, stop=False),
                    reads=("bdi", kar), war_reads=(k_a2r,), writes=(ki,))
                P.step("tensor", lambda e, s=s, pi=pi: e.matmul(
                    pi, bd_s["bdr"][:], A2i[:, s], start=False, stop=True),
                    reads=("bdr", kai), war_reads=(k_a2i,), writes=(ki,))
                evac(0, Xr[:, s], pr, (kr,), (k_xr,), tag + "e2r")
                evac(1, Xi[:, s], pi, (ki,), (k_xi,), tag + "e2i")

        def fwd_transform(src_slot_keys, src_ap_fn, iA1, iA2, iSP, tag):
            for _ in fwd_gen(lambda b: src_slot_keys, src_ap_fn, iA1, iA2,
                             iSP, tag):
                pass

        def inv_s1_shuffle(Yr_ap, Yi_ap, yr_keys, yi_keys, iB, iB2, iB3,
                           real_in, tag):
            Br, Bi = SL[iB[0]], SL[iB[1]]
            B2r, B2i = SL[iB2[0]], SL[iB2[1]]
            B3r, B3i = SL[iB3[0]], SL[iB3[1]]
            k_br, k_bi = f"sl{iB[0]}", f"sl{iB[1]}"
            k_b2r, k_b2i = f"sl{iB2[0]}", f"sl{iB2[1]}"
            k_b3r, k_b3i = f"sl{iB3[0]}", f"sl{iB3[1]}"
            ROT_RE = (4, 5, 0, 1)
            ROT_IM = (6, 7, 2, 3)
            for ch in range(FD // 512):
                s = slice(ch * 512, (ch + 1) * 512)
                pr, kr = bank[ROT_RE[ch % 4]][:, 0:512], BK[ROT_RE[ch % 4]]
                pi, ki = bank[ROT_IM[ch % 4]][:, 0:512], BK[ROT_IM[ch % 4]]
                if real_in:
                    P.step("tensor", lambda e, s=s, pr=pr: e.matmul(
                        pr, bd_s["bdr"][:], Yr_ap[:, s], start=True, stop=True),
                        reads=("bdr",) + yr_keys, writes=(kr,))
                    P.step("tensor", lambda e, s=s, pi=pi: e.matmul(
                        pi, bd_s["bii"][:], Yr_ap[:, s], start=True, stop=True),
                        reads=("bii",) + yr_keys, writes=(ki,))
                else:
                    P.step("tensor", lambda e, s=s, pr=pr: e.matmul(
                        pr, bd_s["bdr"][:], Yr_ap[:, s], start=True, stop=False),
                        reads=("bdr",) + yr_keys, writes=(kr,))
                    P.step("tensor", lambda e, s=s, pr=pr: e.matmul(
                        pr, bd_s["bii_n"][:], Yi_ap[:, s], start=False, stop=True),
                        reads=("bii_n",) + yi_keys, writes=(kr,))
                    P.step("tensor", lambda e, s=s, pi=pi: e.matmul(
                        pi, bd_s["bii"][:], Yr_ap[:, s], start=True, stop=False),
                        reads=("bii",) + yr_keys, writes=(ki,))
                    P.step("tensor", lambda e, s=s, pi=pi: e.matmul(
                        pi, bd_s["bdr"][:], Yi_ap[:, s], start=False, stop=True),
                        reads=("bdr",) + yi_keys, writes=(ki,))
                evac(0, Br[:, s], pr, (kr,), (k_br, k_br + f"_c{ch}"), tag + "e1r")
                evac(1, Bi[:, s], pi, (ki,), (k_bi, k_bi + f"_c{ch}"), tag + "e1i")
            for p4 in range(4):
                osl4 = slice(p4 * 2048, (p4 + 1) * 2048)
                krd = tuple(k_br + f"_c{c}" for c in range(4 * p4, 4 * p4 + 4))
                kid = tuple(k_bi + f"_c{c}" for c in range(4 * p4, 4 * p4 + 4))
                P.step("vector", lambda e, osl4=osl4: e.transpose(
                    B2r[:, osl4], Br[:, osl4]),
                    reads=krd, war_reads=(k_br,),
                    writes=(k_b2r,), risky=True, note=tag + "vt_r")
                P.step("vector", lambda e, osl4=osl4: e.transpose(
                    B2i[:, osl4], Bi[:, osl4]),
                    reads=kid, war_reads=(k_bi,),
                    writes=(k_b2i,), risky=True, note=tag + "vt_i")
            srcr = B2r[:].rearrange("p (r b) -> p b r", b=NB)
            dstr = B3r[:].rearrange("p (b r) -> p b r", b=NB)
            srci = B2i[:].rearrange("p (r b) -> p b r", b=NB)
            dsti = B3i[:].rearrange("p (b r) -> p b r", b=NB)
            for q in range(4):
                bq = slice(q * 8, (q + 1) * 8)
                P.step("vector", lambda e, bq=bq, srcr=srcr, dstr=dstr: e.tensor_copy(
                    dstr[:, bq, :], srcr[:, bq, :]),
                    reads=(k_b2r,), writes=(k_b3r, k_b3r + f"_q{q}"),
                    note=tag + "cpv")
                P.step("gpsimd", lambda e, bq=bq, srci=srci, dsti=dsti: e.tensor_copy(
                    dsti[:, bq, :], srci[:, bq, :]),
                    reads=(k_b2i,), writes=(k_b3i, k_b3i + f"_q{q}"),
                    note=tag + "cpp")
            return (k_b3r, k_b3i)

        xv = xs1b[:].rearrange("p (b r) -> p b r", b=NB)

        def emit_x_load(rb, half):
            bsl = slice(half * 16, half * 16 + 16)
            P.step("gpsimd", lambda e, rb=rb, bsl=bsl: e.dma_start(
                out=xs1b[:].rearrange("p (b r) -> p b r", b=NB)[:, bsl, :],
                in_=dr["xs1"][:, bsl, rb * RB:(rb + 1) * RB]),
                writes=(f"xs1_h{half}",), dma=True, dsem=f"dX{half}",
                note=f"rb{rb}_ld_x{half}")

        def xsrc(b):
            return xv[:, b, :]

        def xkeys(b):
            return (f"xs1_h{b // 16}",)

        gx = None
        for rb in range(NRB):
            tag = f"rb{rb}_"
            if gx is None:
                emit_x_load(rb, 0)
                emit_x_load(rb, 1)
                gx = fwd_gen(xkeys, xsrc, (0, 1), (2, 3), (4, 5), tag + "fx")
                next(gx)
            for _ in gx:
                pass
            gx = None
            if dev_taps and rb == 0:
                t = mk_tap("Xr", [128, FD], BF16)
                P.step("gpsimd", lambda e, t=t: e.dma_start(out=t[:], in_=SL[4][:]),
                       reads=("sl4",), dma=True, dsem="dTP")
                t = mk_tap("Xi", [128, FD], BF16)
                P.step("gpsimd", lambda e, t=t: e.dma_start(out=t[:], in_=SL[5][:]),
                       reads=("sl5",), dma=True, dsem="dTP")
            for ch in range(4):
                s = slice(ch * 2048, (ch + 1) * 2048)
                P.step("scalar", lambda e, s=s: e.activation(SL[0][:, s], SL[4][:, s], AF.Square),
                       reads=("sl4",), writes=("sl0",))
                P.step("scalar", lambda e, s=s: e.activation(SL[1][:, s], SL[5][:, s], AF.Square),
                       reads=("sl5",), writes=("sl1",))
                P.step("vector", lambda e, s=s: e.tensor_add(SL[6][:, s], SL[0][:, s], SL[1][:, s]),
                       reads=("sl0", "sl1"), writes=("sl6",))
            inv_s1_shuffle(SL[6][:], None, ("sl6",), (), (0, 1), (2, 3), (6, 7),
                           True, tag + "ci")
            b3rv = SL[6][:].rearrange("p (b r) -> p b r", b=NB)
            b3iv = SL[7][:].rearrange("p (b r) -> p b r", b=NB)
            for b in range(NB):
                h = (b % 2) * RB
                ib = (b // 2) % 3
                pr, kr = bank[ib][:, h:h + RB], BK[ib]
                P.step("tensor", lambda e, b=b, pr=pr: e.matmul(
                    pr, m2r_b(b), b3rv[:, b, :], start=True, stop=False),
                    reads=("m2r", f"sl6_q{b // 8}"), war_reads=("sl6",),
                    writes=(kr,))
                P.step("tensor", lambda e, b=b, pr=pr: e.matmul(
                    pr, m2i_nb(b), b3iv[:, b, :], start=False, stop=True),
                    reads=("m2i_n", f"sl7_q{b // 8}"), war_reads=("sl7",),
                    writes=(kr,))
                if b % 2 == 1:
                    prf = bank[ib][:, 0:2 * RB]
                    evac(0, SL[0][:, (b - 1) * RB:(b + 1) * RB], prf, (kr,),
                         ("sl0",), "cFM")
                jb = 3 + (b % 3)
                for rs in range(2):
                    pt, kt = bank[jb][:, rs * 128:(rs + 1) * 128], BK[jb]
                    P.step("tensor", lambda e, b=b, rs=rs, pt=pt: e.matmul(
                        pt, b3rv[:, b, rs * 128:(rs + 1) * 128], m2r_b(b),
                        start=True, stop=False),
                        reads=("m2r", f"sl6_q{b // 8}"), war_reads=("sl6",),
                        writes=(kt,))
                    P.step("tensor", lambda e, b=b, rs=rs, pt=pt: e.matmul(
                        pt, b3iv[:, b, rs * 128:(rs + 1) * 128], m2i_nb(b),
                        start=False, stop=True),
                        reads=("m2i_n", f"sl7_q{b // 8}"), war_reads=("sl7",),
                        writes=(kt,))
                ptf = bank[jb][:, 0:256]
                dstTM = SL[1][:].rearrange("p (rs q) -> p rs q", rs=2)[
                    :, :, b * 128:(b + 1) * 128]
                P.step("vector", lambda e, b=b, ptf=ptf, dstTM=dstTM: e.tensor_copy(
                    dstTM, ptf[:].rearrange("p (rs q) -> p rs q", rs=2)),
                    reads=(BK[jb],), writes=("sl1",), note="cTM")
            if dev_taps and rb == 0:
                t = mk_tap("corrFM", [128, FD], BF16)
                P.step("gpsimd", lambda e, t=t: e.dma_start(out=t[:], in_=SL[0][:]),
                       reads=("sl0",), dma=True, dsem="dTP")
                t = mk_tap("corrTM", [128, FD], BF16)
                P.step("gpsimd", lambda e, t=t: e.dma_start(out=t[:], in_=SL[1][:]),
                       reads=("sl1",), dma=True, dsem="dTP")
            for rs in range(2):
                P.step("vector", lambda e, rs=rs: e.max(
                    out=m8[:, rs * 8:rs * 8 + 8], in_=SL[1][:, rs * 4096:(rs + 1) * 4096]),
                    reads=("sl1",), writes=("m8",), risky=True)
            for rs in range(2):
                pt, kt = bank[6][0:1, 0:128], BK[6]
                P.step("tensor", lambda e, rs=rs, pt=pt: e.transpose(
                    pt, m8[:, rs * 8:rs * 8 + 1], idf[:]),
                    reads=("m8", "idf"), writes=(kt,), note="tr_m1")
                P.step("vector", lambda e, rs=rs, pt=pt: e.tensor_copy(
                    mrow[0:1, rs * 128:(rs + 1) * 128], pt),
                    reads=(kt,), writes=("mrow",))
                pt2, kt2 = bank[7][0:1, 0:128], BK[7]
                P.step("tensor", lambda e, rs=rs, pt2=pt2: e.transpose(
                    pt2, m8[:, rs * 8 + 7:rs * 8 + 8], idf[:]),
                    reads=("m8", "idf"), writes=(kt2,), note="tr_t8")
                P.step("vector", lambda e, rs=rs, pt2=pt2: e.tensor_copy(
                    trow[0:1, rs * 128:(rs + 1) * 128], pt2),
                    reads=(kt2,), writes=("trow",))
            pm, km = bank[6][:, 0:RB], BK[6]
            P.step("tensor", lambda e, pm=pm: e.matmul(
                pm, ones_row[:], mrow[0:1, 0:RB], start=True, stop=True),
                reads=("mrow", "ones_row"), writes=(km,))
            P.step("vector", lambda e, pm=pm: e.tensor_copy(m1b[:], pm),
                   reads=(km,), writes=("m1b",))
            pm2, km2 = bank[7][:, 0:RB], BK[7]
            P.step("tensor", lambda e, pm2=pm2: e.matmul(
                pm2, ones_row[:], trow[0:1, 0:RB], start=True, stop=True),
                reads=("trow", "ones_row"), writes=(km2,))
            P.step("vector", lambda e, pm2=pm2: e.tensor_sub(ttb[:], pm2, m1b[:]),
                   reads=(km2, "m1b"), writes=("ttb",))
            m1bb = m1b[:].unsqueeze(1).to_broadcast([128, NB, RB])
            ttbb = ttb[:].unsqueeze(1).to_broadcast([128, NB, RB])
            cfmv = SL[0][:].rearrange("p (b r) -> p b r", b=NB)
            s2v = SL[2][:].rearrange("p (b r) -> p b r", b=NB)
            s3v = SL[3][:].rearrange("p (b r) -> p b r", b=NB)
            P.step("vector", lambda e: e.tensor_tensor(s3v, cfmv, m1bb, ALU.subtract),
                   reads=("sl0", "m1b"), writes=("sl3",), note="g_d")
            P.step("vector", lambda e: e.tensor_tensor(s2v, s3v, ttbb, ALU.is_ge),
                   reads=("sl3", "ttb"), writes=("sl2",), note="g_mask")
            P.step("scalar", lambda e: e.activation(SL[6][:], SL[3][:], AF.Exp),
                   reads=("sl3",), writes=("sl6",), note="g_exp")
            P.step("vector", lambda e: e.tensor_mul(SL[2][:], SL[2][:], SL[6][:]),
                   reads=("sl2", "sl6"), writes=("sl2",), note="g_mul")
            if dev_taps and rb == 0:
                t = mk_tap("g", [128, FD], BF16)
                P.step("gpsimd", lambda e, t=t: e.dma_start(out=t[:], in_=SL[2][:]),
                       reads=("sl2",), dma=True, dsem="dTP")
            psm, ksm = bank[7][0:1, 0:RB], BK[7]
            g2v = SL[2][:].rearrange("p (b r) -> p b r", b=NB)
            for b in range(NB):
                P.step("tensor", lambda e, b=b, psm=psm: e.matmul(
                    psm, ones_col[:], g2v[:, b, :], start=(b == 0), stop=(b == NB - 1)),
                    reads=("sl2", "ones_col"), writes=(ksm,))
            P.step("vector", lambda e, psm=psm: e.reciprocal(srow[0:1, 0:RB], psm),
                   reads=(ksm,), writes=("srow",), risky=True)
            pm3, km3 = bank[6][:, 0:RB], BK[6]
            P.step("tensor", lambda e, pm3=pm3: e.matmul(
                pm3, ones_row[:], srow[0:1, 0:RB], start=True, stop=True),
                reads=("srow", "ones_row"), writes=(km3,))
            P.step("vector", lambda e, pm3=pm3: e.tensor_copy(rsb[:], pm3),
                   reads=(km3,), writes=("rsb",))
            def gsrc(b):
                return g2v[:, b, :]

            fwd_transform(("sl2",), gsrc, (1, 3), (0, 6), (7, 2), tag + "fg")
            for ch in range(4):
                s = slice(ch * 2048, (ch + 1) * 2048)
                P.step("vector", lambda e, s=s: e.tensor_mul(SL[0][:, s], SL[4][:, s], SL[7][:, s]),
                       reads=("sl4", "sl7"), writes=("sl0",), note="y_t1")
                P.step("gpsimd", lambda e, s=s: e.tensor_mul(SL[1][:, s], SL[5][:, s], SL[2][:, s]),
                       reads=("sl5", "sl2"), writes=("sl1",), note="y_t2")
                P.step("vector", lambda e, s=s: e.tensor_mul(SL[3][:, s], SL[5][:, s], SL[7][:, s]),
                       reads=("sl5", "sl7"), writes=("sl3",), note="y_t3")
                P.step("gpsimd", lambda e, s=s: e.tensor_mul(SL[6][:, s], SL[4][:, s], SL[2][:, s]),
                       reads=("sl4", "sl2"), writes=("sl6",), note="y_t4")
                P.step("vector", lambda e, s=s: e.tensor_add(SL[0][:, s], SL[0][:, s], SL[1][:, s]),
                       reads=("sl0", "sl1"), writes=("sl0",), note="y_Yr")
                P.step("vector", lambda e, s=s: e.tensor_sub(SL[1][:, s], SL[3][:, s], SL[6][:, s]),
                       reads=("sl3", "sl6"), writes=("sl1",), note="y_Yi")
            inv_s1_shuffle(SL[0][:], SL[1][:], ("sl0",), ("sl1",), (2, 7), (3, 6),
                           (0, 1), False, tag + "yi")
            yb3rv = SL[0][:].rearrange("p (b r) -> p b r", b=NB)
            yb3iv = SL[1][:].rearrange("p (b r) -> p b r", b=NB)
            residv = SL[4][:].rearrange("p (b r) -> p b r", b=NB)
            for b in range(NB):
                pr, kr = bank[b % 3][:, 0:RB], BK[b % 3]
                P.step("tensor", lambda e, b=b, pr=pr: e.matmul(
                    pr, m2r_b(b), yb3rv[:, b, :], start=True, stop=False),
                    reads=("m2r", f"sl0_q{b // 8}"), war_reads=("sl0",),
                    writes=(kr,))
                P.step("tensor", lambda e, b=b, pr=pr: e.matmul(
                    pr, m2i_nb(b), yb3iv[:, b, :], start=False, stop=True),
                    reads=("m2i_n", f"sl1_q{b // 8}"), war_reads=("sl1",),
                    writes=(kr,))
                P.step("vector", lambda e, b=b, pr=pr: e.tensor_mul(
                    SL[3][:, b * RB:(b + 1) * RB], pr, rsb[:]),
                    reads=(kr, "rsb"), writes=("sl3",), note="res_y1")
                P.step("vector", lambda e, b=b: e.tensor_add(
                    residv[:, b, :], SL[3][:, b * RB:(b + 1) * RB], xv[:, b, :]),
                    reads=("sl3", f"xs1_h{b // 16}"), writes=(f"sl4b{b}",),
                    note="res_add")
                if rb + 1 < NRB and b in (15, NB - 1):
                    emit_x_load(rb + 1, b // 16)
            if dev_taps and rb == 0:
                t = mk_tap("resid", [128, FD], BF16)
                P.step("gpsimd", lambda e, t=t: e.dma_start(out=t[:], in_=SL[4][:]),
                       reads=tuple(f"sl4b{b}" for b in range(NB)),
                       dma=True, dsem="dTP")
            if rb + 1 < NRB:
                gx = fwd_gen(xkeys, xsrc, (0, 1), (2, 3), (4, 5),
                             f"rb{rb + 1}_fx")
                next(gx)
            psm1, ksm1 = bank[6][0:1, 0:RB], BK[6]
            psq1, ksq1 = bank[7][0:1, 0:RB], BK[7]
            for b in range(NB):
                P.step("tensor", lambda e, b=b, psm1=psm1: e.matmul(
                    psm1, ones_col[:], residv[:, b, :], start=(b == 0),
                    stop=(b == NB - 1)),
                    reads=(f"sl4b{b}", "ones_col"), war_reads=("sl4",),
                    writes=(ksm1,))
                P.step("scalar", lambda e, b=b: e.activation(
                    SL[6][:, b * RB:(b + 1) * RB], residv[:, b, :], AF.Square),
                    reads=(f"sl4b{b}",), war_reads=("sl4",),
                    writes=(f"sl6b{b}",) + (("sl6",) if b == 0 else ()),
                    note="res_sq")
                P.step("tensor", lambda e, b=b, psq1=psq1: e.matmul(
                    psq1, ones_col[:], SL[6][:, b * RB:(b + 1) * RB],
                    start=(b == 0), stop=(b == NB - 1)),
                    reads=(f"sl6b{b}", "ones_col"), war_reads=("sl6",),
                    writes=(ksq1,))
            P.step("vector", lambda e, psm1=psm1: e.tensor_scalar_mul(
                srow[0:1, 0:RB], psm1, 1.0 / N),
                reads=(ksm1,), writes=("srow",))
            P.step("vector", lambda e, psq1=psq1: e.tensor_scalar_mul(
                srow2[0:1, 0:RB], psq1, 1.0 / N),
                reads=(ksq1,), writes=("srow2",))
            P.step("vector", lambda e: e.tensor_mul(srow3[0:1, 0:RB], srow[0:1, 0:RB],
                                                    srow[0:1, 0:RB]),
                   reads=("srow",), writes=("srow3",))
            P.step("vector", lambda e: e.tensor_sub(srow2[0:1, 0:RB], srow2[0:1, 0:RB],
                                                    srow3[0:1, 0:RB]),
                   reads=("srow2", "srow3"), writes=("srow2",))
            P.step("vector", lambda e: e.tensor_scalar_add(
                srow2[0:1, 0:RB], srow2[0:1, 0:RB], EPS),
                reads=("srow2",), writes=("srow2",))
            P.step("scalar", lambda e: e.activation(srow2[0:1, 0:RB], srow2[0:1, 0:RB],
                                                    AF.Sqrt),
                   reads=("srow2",), writes=("srow2",))
            P.step("vector", lambda e: e.reciprocal(srow2[0:1, 0:RB], srow2[0:1, 0:RB]),
                   reads=("srow2",), writes=("srow2",), risky=True)
            if gx is not None:
                next(gx)
            pmB, kmB = bank[6][:, 0:RB], BK[6]
            P.step("tensor", lambda e, pmB=pmB: e.matmul(
                pmB, ones_row[:], srow[0:1, 0:RB], start=True, stop=True),
                reads=("srow", "ones_row"), writes=(kmB,))
            P.step("vector", lambda e, pmB=pmB: e.tensor_copy(mb1[:], pmB),
                   reads=(kmB,), writes=("mb1",))
            pmC, kmC = bank[7][:, 0:RB], BK[7]
            P.step("tensor", lambda e, pmC=pmC: e.matmul(
                pmC, ones_row[:], srow2[0:1, 0:RB], start=True, stop=True),
                reads=("srow2", "ones_row"), writes=(kmC,))
            P.step("vector", lambda e, pmC=pmC: e.tensor_copy(rb1[:], pmC),
                   reads=(kmC,), writes=("rb1",))
            sst4 = sstage + [SL[5][:, 0:512].bitcast(F32),
                             SL[5][:, 512:1024].bitcast(F32)]
            for b in range(NB):
                st = sst4[b % 4]
                kst = f"sst{b % 4}"
                wst = (kst, "sl5") if b in (2, 3) else (kst,)
                wrn = ("sl4",) if b % 4 < 2 else ("sl4", "sl5")
                P.step("vector", lambda e, b=b, st=st: e.tensor_sub(
                    st[:], residv[:, b, :], mb1[:]),
                    reads=(f"sl4b{b}", "mb1"), war_reads=("sl4",), writes=wst)
                P.step("vector", lambda e, st=st: e.tensor_mul(
                    st[:], st[:], rb1[:]),
                    reads=("rb1",), writes=(kst,))
                P.step("sync" if b % 2 == 0 else "gpsimd",
                       lambda e, b=b, st=st, rb=rb: e.dma_start(
                           out=dr["sdram"][b, :, rb * RB:(rb + 1) * RB], in_=st[:]),
                       reads=(kst,), war_reads=wrn[1:],
                       writes=(f"sd_{b}_{rb}",), dma=True,
                       dsem="dST" if b % 2 == 0 else "dST2")

        P.step("vector", lambda e: e.memset(SL[7][:, 0:2], 0.0),
               writes=("sl7", "sl7_mb", "sl7_rb", "sl7_os00", "sl7_os01",
                       "sl7_os10", "sl7_os11", "sl7_sq0", "sl7_sq1", "sl7_lf"),
               note="sl7_bridge")
        schunk4 = [schunk[0], schunk[1],
                   SL[4][:, 0:2048].bitcast(F32),
                   SL[4][:, 2048:4096].bitcast(F32)]
        for blk in range(2):
            fsl = slice(blk * FBLK, (blk + 1) * FBLK)
            routT, kroutT = routTs[blk], f"routT{blk}"
            psL, kL = bank[5][0:8, 0:512], BK[5]
            for b in range(NB):
                pair = b // 2
                sc = schunk4[pair % 4]
                ksc = f"sch{pair % 4}"
                wsc = (ksc, "sl4") if (blk == 0 and pair in (2, 3)) else (ksc,)
                if b % 2 == 0:
                    P.step("sync" if pair % 2 == 0 else "gpsimd",
                           lambda e, b=b, sc=sc, fsl=fsl: e.dma_start(
                        out=sc[:].rearrange("p (b r) -> p b r", b=2),
                        in_=dr["sdram"][b:b + 2, :, fsl].rearrange("b p r -> p b r")),
                        reads=(f"sd_{b}_{2 * blk}", f"sd_{b}_{2 * blk + 1}",
                               f"sd_{b + 1}_{2 * blk}", f"sd_{b + 1}_{2 * blk + 1}"),
                        writes=wsc, dma=True, dsem=f"dSC{pair % 4}")
                P.step("tensor", lambda e, b=b, sc=sc, psL=psL: e.matmul(
                    psL, gate_s[:, b * 8:(b + 1) * 8],
                    sc[:, (b % 2) * 512:(b % 2 + 1) * 512],
                    start=(b == 0), stop=(b == NB - 1)),
                    reads=(ksc, "gate"), writes=(kL,))
            P.step("vector", lambda e, psL=psL: e.tensor_copy(logF[:], psL),
                   reads=(kL,), writes=("sl7_lf",))
            for sub in range(4):
                pt, kt = bank[6][:, 0:8], BK[6]
                P.step("tensor", lambda e, sub=sub, pt=pt: e.transpose(
                    pt, logF[0:8, sub * 128:(sub + 1) * 128], idf[0:8, 0:8]),
                    reads=("sl7_lf", "idf"), writes=(kt,))
                P.step("vector", lambda e, sub=sub, pt=pt: e.tensor_copy(
                    logT[:, sub * 8:(sub + 1) * 8], pt),
                    reads=(kt,), writes=("logT",))
            for sub in range(4):
                lsl = slice(sub * 8, (sub + 1) * 8)
                P.step("vector", lambda e, lsl=lsl: e.reduce_max(
                    negmax[:], logT[:, lsl], axis=AX, negate=True),
                    reads=("logT",), writes=("negmax",), risky=True)
                P.step("scalar", lambda e, lsl=lsl: e.activation(
                    routTM[:, 0:8], logT[:, lsl], AF.Exp, bias=negmax[:, 0:1]),
                    reads=("logT", "negmax"), writes=("routTM",))
                P.step("vector", lambda e: e.reduce_sum(essum[:], routTM[:, 0:8], axis=AX),
                       reads=("routTM",), writes=("essum",), risky=True)
                P.step("vector", lambda e: e.reciprocal(essum[:], essum[:]),
                       reads=("essum",), writes=("essum",), risky=True)
                P.step("vector", lambda e: e.tensor_scalar_mul(
                    routTM[:, 0:8], routTM[:, 0:8], essum[:, 0:1]),
                    reads=("essum",), writes=("routTM",))
                pt2, kt2 = bank[6][0:8, 8:136], BK[6]
                P.step("tensor", lambda e, pt2=pt2: e.transpose(
                    pt2, routTM[:, 0:8], idf[:]),
                    reads=("routTM", "idf"), writes=(kt2,))
                P.step("vector", lambda e, sub=sub, pt2=pt2, routT=routT,
                       kroutT=kroutT: e.tensor_copy(
                    routT[:, sub * 128:(sub + 1) * 128], pt2),
                    reads=(kt2,), writes=(kroutT,))
            for b in range(NB):
                ic, kic = ics_t[b % 2], f"ics{b % 2}"
                P.step("sync", lambda e, b=b, ic=ic: e.dma_start(
                    out=ic[:], in_=dr["icsp"][b]),
                    writes=(kic,), dma=True, dsem=f"dIC{b % 2}")
                pair = b // 2
                sc = schunk4[pair % 4]
                ksc = f"sch{pair % 4}"
                if b % 2 == 0:
                    P.step("sync" if pair % 2 == 0 else "gpsimd",
                           lambda e, b=b, sc=sc, fsl=fsl: e.dma_start(
                        out=sc[:].rearrange("p (b r) -> p b r", b=2),
                        in_=dr["sdram"][b:b + 2, :, fsl].rearrange("b p r -> p b r")),
                        reads=(f"sd_{b}_{2 * blk}", f"sd_{b}_{2 * blk + 1}",
                               f"sd_{b + 1}_{2 * blk}", f"sd_{b + 1}_{2 * blk + 1}"),
                        writes=(ksc,), dma=True, dsem=f"dSC{pair % 4}")
                psI, kI = bank[b % 4][:, 0:512], BK[b % 4]
                P.step("tensor", lambda e, ic=ic, psI=psI, routT=routT: e.matmul(
                    psI, ic[:], routT[:], start=True, stop=True),
                    reads=(kic, kroutT), writes=(kI,))
                us = 2 * blk + b // 16
                ub = SL[us][:].rearrange("p (q r) -> p q r", q=16)[:, b % 16, :]
                P.step("vector", lambda e, b=b, ub=ub, sc=sc, psI=psI: e.tensor_mul(
                    ub, sc[:, (b % 2) * 512:(b % 2 + 1) * 512], psI),
                    reads=(ksc, kI), writes=(f"sl{us}",), note="u")

        def r2view(blk, oc):
            if blk == 0:
                sl, key = SL[5 + oc // 16], f"sl{5 + oc // 16}"
            elif oc < 16:
                sl, key = SL[4], "sl4"
            else:
                sl, key = xs1b, "xs1"
            return sl[:].rearrange("p (q r) -> p q r", q=16)[:, oc % 16, :], key

        def emit_wt_load(oc):
            if oc >= 32:
                return
            wt = m2r_s if oc % 2 == 0 else m2i_ns
            kwt = "m2r" if oc % 2 == 0 else "m2i_n"
            P.step("gpsimd", lambda e, oc=oc, wt=wt: e.dma_start(
                out=wt[:], in_=dr["wth"][oc]),
                writes=(kwt,), dma=True, dsem=f"dWT{oc % 2}", note="wt_ld")

        emit_wt_load(0)
        emit_wt_load(1)
        FROT = ((4, 5, 6), (2, 3, 7))
        for oc in range(32):
            wt = m2r_s if oc % 2 == 0 else m2i_ns
            kwt = "m2r" if oc % 2 == 0 else "m2i_n"
            wbv = m1r_s if oc % 2 == 0 else m1i_s
            kwb = "m1r" if oc % 2 == 0 else "m1i"
            P.step("scalar", lambda e, wt=wt, wbv=wbv: e.sign(wbv[:], wt[:]),
                   reads=(kwt,), writes=(kwb,), note="sign")
            emit_wt_load(oc + 2)
            ocb, koc = ocs_t[oc % 2], f"ocs{oc % 2}"
            P.step("sync", lambda e, oc=oc, ocb=ocb: e.dma_start(
                out=ocb[:], in_=dr["ocsp"][oc]),
                writes=(koc,), dma=True, dsem=f"dOC{oc % 2}")
            for blk in range(2):
                fsl = slice(blk * FBLK, (blk + 1) * FBLK)
                psO, kO = bank[blk][:, 0:512], BK[blk]
                P.step("tensor", lambda e, ocb=ocb, psO=psO, blk=blk: e.matmul(
                    psO, ocb[:], routTs[blk][:], start=True, stop=True),
                    reads=(koc, f"routT{blk}"), writes=(kO,))
                osb = os_sb[blk][oc % 2]
                kos = f"sl7_os{blk}{oc % 2}"
                P.step("vector", lambda e, psO=psO, osb=osb: e.tensor_copy(osb[:], psO),
                       reads=(kO,), writes=(kos,))
                snb = schunk[oc % 2][:, blk * 512:(blk + 1) * 512]
                ksn = f"sn{oc % 2}{blk}"
                wsn = (ksn, f"sch{oc % 2}") if oc < 2 else (ksn,)
                P.step("sync", lambda e, oc=oc, snb=snb, fsl=fsl: e.dma_start(
                    out=snb,
                    in_=dr["sdram"][:, oc * 4:(oc + 1) * 4, fsl].rearrange(
                        "b a r -> a b r")),
                    reads=tuple(f"sd_{b}_{r}" for b in range(NB)
                                for r in (2 * blk, 2 * blk + 1)),
                    writes=wsn, dma=True, dsem=f"dSC{oc % 2}", note="s_nat")
            for blk in range(2):
                psF, kF = bank[FROT[blk][oc % 3]][:, 0:512], BK[FROT[blk][oc % 3]]
                for b in range(NB):
                    us = 2 * blk + b // 16
                    ub = SL[us][:].rearrange("p (q r) -> p q r", q=16)[:, b % 16, :]
                    P.step("tensor", lambda e, b=b, wbv=wbv, ub=ub, psF=psF: e.matmul(
                        psF, wbv[:, b * 128:(b + 1) * 128], ub,
                        start=(b == 0), stop=(b == NB - 1)),
                        reads=(kwb, f"sl{us}"), writes=(kF,), note="ffn_mm")
                osb = os_sb[blk][oc % 2]
                kos = f"sl7_os{blk}{oc % 2}"
                r2v, kr2 = r2view(blk, oc)
                P.step("vector", lambda e, psF=psF, osb=osb, r2v=r2v: e.tensor_mul(
                    r2v, psF, osb[:]),
                    reads=(kF, kos), writes=(kr2,), note="f_os")
                snb = schunk[oc % 2][:, blk * 512:(blk + 1) * 512]
                ksn = f"sn{oc % 2}{blk}"
                P.step("vector", lambda e, oc=oc, snb=snb, r2v=r2v: e.scalar_tensor_tensor(
                    r2v, r2v, bias_s[:, oc:oc + 1], snb,
                    op0=ALU.add, op1=ALU.add),
                    reads=(ksn, "bias", kr2), writes=(kr2,), note="f_bias_res")

        psm2, ksm2 = bank[7][0:1, 0:512], BK[7]
        psq2, kq2 = bank[2][0:1, 0:512], BK[2]

        def ln2_stats(blk):
            for oc in range(32):
                r2v, kr2 = r2view(blk, oc)
                P.step("tensor", lambda e, oc=oc, r2v=r2v, psm2=psm2: e.matmul(
                    psm2, ones_col[:], r2v, start=(oc == 0), stop=(oc == 31)),
                    reads=(kr2, "ones_col"), writes=(ksm2,))
                sq = sqtmp2[oc % 2]
                ksq = f"sl7_sq{oc % 2}"
                P.step("scalar", lambda e, sq=sq, r2v=r2v: e.activation(sq, r2v, AF.Square),
                       reads=(kr2,), writes=(ksq,))
                P.step("tensor", lambda e, oc=oc, sq=sq, psq2=psq2: e.matmul(
                    psq2, ones_col[:], sq, start=(oc == 0), stop=(oc == 31)),
                    reads=(ksq, "ones_col"), writes=(kq2,))

        def ln2_scalars():
            P.step("vector", lambda e, psm2=psm2: e.tensor_scalar_mul(
                srow[0:1, :], psm2, 1.0 / N), reads=(ksm2,), writes=("srow",))
            P.step("vector", lambda e, psq2=psq2: e.tensor_scalar_mul(
                srow2[0:1, :], psq2, 1.0 / N), reads=(kq2,), writes=("srow2",))
            P.step("vector", lambda e: e.tensor_mul(srow3[0:1, :], srow[0:1, :],
                                                    srow[0:1, :]),
                   reads=("srow",), writes=("srow3",))
            P.step("vector", lambda e: e.tensor_sub(srow2[0:1, :], srow2[0:1, :],
                                                    srow3[0:1, :]),
                   reads=("srow2", "srow3"), writes=("srow2",))
            P.step("vector", lambda e: e.tensor_scalar_add(
                srow2[0:1, :], srow2[0:1, :], EPS),
                reads=("srow2",), writes=("srow2",))
            P.step("scalar", lambda e: e.activation(srow2[0:1, :], srow2[0:1, :],
                                                    AF.Sqrt),
                   reads=("srow2",), writes=("srow2",))
            P.step("vector", lambda e: e.reciprocal(srow2[0:1, :], srow2[0:1, :]),
                   reads=("srow2",), writes=("srow2",), risky=True)
            pmB, kmB = bank[3][:, 0:512], BK[3]
            P.step("tensor", lambda e, pmB=pmB: e.matmul(
                pmB, ones_row[:], srow[0:1, :], start=True, stop=True),
                reads=("srow", "ones_row"), writes=(kmB,))
            P.step("vector", lambda e, pmB=pmB: e.tensor_copy(meanb[:], pmB),
                   reads=(kmB,), writes=("sl7_mb",))
            pmC, kmC = bank[4][:, 0:512], BK[4]
            P.step("tensor", lambda e, pmC=pmC: e.matmul(
                pmC, ones_row[:], srow2[0:1, :], start=True, stop=True),
                reads=("srow2", "ones_row"), writes=(kmC,))
            P.step("vector", lambda e, pmC=pmC: e.tensor_copy(rstdb[:], pmC),
                   reads=(kmC,), writes=("sl7_rb",))

        def ln2_normalize(blk):
            fsl = slice(blk * FBLK, (blk + 1) * FBLK)
            for oc in range(32):
                r2v, kr2 = r2view(blk, oc)
                ot = os_sb[(oc % 4) // 2][oc % 2]
                kot = f"sl7_os{(oc % 4) // 2}{oc % 2}"
                P.step("vector", lambda e, r2v=r2v, ot=ot: e.tensor_sub(
                    ot[:], r2v, meanb[:]),
                    reads=(kr2, "sl7_mb"), writes=(kot,))
                P.step("vector", lambda e, ot=ot: e.tensor_mul(ot[:], ot[:], rstdb[:]),
                       reads=("sl7_rb",), writes=(kot,))
                P.step("sync" if oc % 2 == 0 else "gpsimd",
                       lambda e, oc=oc, ot=ot, fsl=fsl: e.dma_start(
                           out=dr["outp"][oc, :, fsl], in_=ot[:]),
                       reads=(kot,), writes=(f"out_{oc}_{blk}",), dma=True,
                       dsem="dOT" if oc % 2 == 0 else "dOT2")

        ln2_stats(0)
        ln2_scalars()
        ln2_stats(1)
        ln2_normalize(0)
        ln2_scalars()
        ln2_normalize(1)

        mx = {}
        for oc in range(32):
            for blk in range(2):
                s, v = P.bufs[f"out_{oc}_{blk}"]["w"]
                mx[s] = max(mx.get(s, 0), v)
        P.step("gpsimd", lambda e: e.nop(), extra_wait=tuple(mx.items()),
               note="final_wait")

        @block.sync
        def _(e):
            P.replay("sync", e)

        @block.gpsimd
        def _(e):
            P.replay("gpsimd", e)

        @block.tensor
        def _(e):
            P.replay("tensor", e)

        @block.vector
        def _(e):
            P.replay("vector", e)

        @block.scalar
        def _(e):
            P.replay("scalar", e)

    return nc, taps


def _host_prep(L0_attn_x, L0_ffn_weight, L0_ffn_bias, L0_ffn_gate_w,
               L0_ffn_in_channel_scale, L0_ffn_out_channel_scale):
    x = np.ascontiguousarray(np.asarray(L0_attn_x, np.float32).reshape(8192, N))
    w = np.asarray(L0_ffn_weight, np.float32)
    bias = np.asarray(L0_ffn_bias, np.float32)
    gate = np.asarray(L0_ffn_gate_w, np.float32)
    ics = np.asarray(L0_ffn_in_channel_scale, np.float32)
    ocs = np.asarray(L0_ffn_out_channel_scale, np.float32)

    wth = np.ascontiguousarray(
        w.reshape(32, 128, 128, 32).transpose(0, 2, 3, 1)).reshape(32, 128, NB * 128)
    gatep = np.ascontiguousarray(gate.T.reshape(128, 32, 8).transpose(1, 0, 2))
    icsp = np.ascontiguousarray(ics.reshape(8, 128, 32).transpose(2, 0, 1))
    ocsp = np.ascontiguousarray(ocs.reshape(8, 32, 128).transpose(1, 0, 2))
    biasc = np.ascontiguousarray(bias.reshape(32, 128).T)
    idf = np.eye(128, dtype=np.float32)
    shared = {"wth": wth, "gatep": gatep, "icsp": icsp, "ocsp": ocsp,
              "biasc": biasc, "idf": idf}
    shared.update(_TABLES)

    in_maps = []
    for ci in range(NCORE):
        xs = x[ci * RPC:(ci + 1) * RPC]
        xs1 = np.ascontiguousarray(xs.reshape(RPC, NA, NB).transpose(1, 2, 0))
        m = dict(shared)
        m["xs1"] = xs1
        in_maps.append(m)
    return in_maps


_PROG_CACHE = {}


def kernel(**inputs):
    if "prog" not in _PROG_CACHE:
        _PROG_CACHE["prog"] = build_program(dev_taps=False)
    nc, _ = _PROG_CACHE["prog"]
    in_maps = _host_prep(**inputs)
    res = run_bass_kernel_spmd(nc, in_maps, core_ids=list(range(NCORE)))
    outs = []
    for ci in range(NCORE):
        o = np.asarray(res.results[ci]["outp"])
        outs.append(o.reshape(N, RPC).T)
    return np.ascontiguousarray(np.concatenate(outs, axis=0).reshape(4, 2048, N))
